# version 13
# baseline (speedup 1.0000x reference)
from contextlib import ExitStack

import numpy as np
import concourse.bass as bass
import concourse.mybir as mybir
from concourse.alu_op_type import AluOpType
from concourse.bass_utils import run_bass_kernel_spmd

F32 = mybir.dt.float32
AF = mybir.ActivationFunctionType
AX = mybir.AxisListType

D = 1024
NCH = 8
SEG = 2050
NSEG = 2
NKEY = 4096
NQ = 2049
NH = 16
HD = 64
EPS = 1e-5
_DEBUG = False

SEG_TILES = [(0, 410), (410, 410), (820, 410), (1230, 410), (1640, 410)]
KEY_TILES = [(2, 512), (514, 512), (1026, 512), (1538, 512)]
Q_TILES = [(0, 410), (410, 410), (820, 410), (1230, 410), (1640, 410)]
QB_LIST = [(1, 410), (411, 410), (821, 410), (1231, 410), (1641, 409)]
F3_TILES = [(0, 410), (410, 410), (820, 410), (1230, 410), (1640, 409)]

ENGINES = ("tensor", "vector", "scalar", "sync", "syncb", "gpsimd")
ENGINE_OF = {"syncb": "sync"}


class Ph:

    def __init__(self):
        self.steps = {e: [] for e in ENGINES}
        self.cnt = {e: 0 for e in ENGINES}
        self.seq = 0

    def snap(self):
        return dict(self.cnt)

    def emit(self, eng, fn, waits=None, inc=None, dr=True):
        if inc is None:
            inc = 16 if eng in ("sync", "syncb", "gpsimd") else 1
        if isinstance(waits, (list, tuple)):
            waits = {e: self.cnt[e] for e in waits if e != eng}
        waits = {e: v for e, v in (waits or {}).items() if v > 0}
        self.cnt[eng] += inc
        self.seq += 1
        self.steps[eng].append((fn, waits, self.cnt[eng], self.seq, dr))

    def realize(self, nc, block, sems):
        waited = {e: set() for e in ENGINES}
        for e in ENGINES:
            for _, waits, _, _, _ in self.steps[e]:
                for we, wv in waits.items():
                    waited[we].add(wv)
        for e in ("sync", "syncb", "gpsimd"):
            if self.cnt[e] > 0:
                waited[e].add(self.cnt[e])
        plans = {}
        for e in ENGINES:
            if e in ("sync", "syncb", "gpsimd"):
                plan = {}
                prev = 0
                for i, (_, _, after, _, _) in enumerate(self.steps[e]):
                    plan[i] = after - prev
                    prev = after
                plans[e] = plan
                continue
            if getattr(self, "dense", False):
                plan = {}
                prev = 0
                for i, (_, _, after, _, _) in enumerate(self.steps[e]):
                    plan[i] = after - prev
                    prev = after
                plans[e] = plan
                continue
            targets = sorted(waited[e])
            plan = {}
            prev = 0
            ti = 0
            for i, (_, _, after, _, _) in enumerate(self.steps[e]):
                amt = 0
                while ti < len(targets) and targets[ti] <= after:
                    amt += targets[ti] - prev
                    prev = targets[ti]
                    ti += 1
                if amt:
                    plan[i] = amt
            plans[e] = plan

        def body(phys):
            merged = []
            for e in ENGINES:
                if ENGINE_OF.get(e, e) == phys:
                    for i, step in enumerate(self.steps[e]):
                        merged.append((step[3], e, i, step))
            merged.sort()

            def run(builder):
                for _, e, i, (fn, waits, _, _, dr) in merged:
                    for we, wv in waits.items():
                        builder.wait_ge(sems[we], wv)
                    inst = fn(builder)
                    if i in plans[e]:
                        assert inst is not None, f"{e} step {i} returned None"
                        inst.then_inc(sems[e], plans[e][i])
                    if dr and phys in ("vector", "scalar"):
                        builder.drain()
                for e in ENGINES:
                    if ENGINE_OF.get(e, e) == phys and \
                            e in ("sync", "syncb", "gpsimd") and self.cnt[e] > 0:
                        builder.wait_ge(sems[e], self.cnt[e])
            return run

        for phys in ("tensor", "vector", "scalar", "sync", "gpsimd"):
            if any(self.steps[e] and ENGINE_OF.get(e, e) == phys
                   for e in ENGINES):
                getattr(block, phys)(body(phys))


def _emit_ln_stats(ph, w, xt, tmp, rows, bankS1, bankS2, onescol128,
                   first_vec_waits=None):
    mu = rows[0:1, 0:w]
    t = rows[0:1, 512:512 + w]
    var = rows[0:1, 1024:1024 + w]
    rstd = rows[0:1, 1536:1536 + w]
    for c in range(NCH):
        xc = xt[:, c * w:(c + 1) * w]
        ph.emit("tensor", lambda b, xc=xc, c=c: b.matmul(
            bankS1[0:1, 0:w], onescol128[:, 0:1], xc,
            start=(c == 0), stop=(c == NCH - 1)),
            waits=(["sync", "vector"] if c == 0 else None))
    s2_done = {}
    for c in range(NCH):
        xc = xt[:, c * w:(c + 1) * w]
        tb = tmp[:, (c % 2) * 512:(c % 2) * 512 + w]
        if c == 0:
            wt = dict(first_vec_waits or {})
            wt["sync"] = ph.snap()["sync"]
        elif c == 1:
            wt = None
        else:
            wt = {"tensor": s2_done[c - 2]}
        ph.emit("vector", lambda b, xc=xc, tb=tb: b.tensor_tensor(
            tb, xc, xc, AluOpType.mult), waits=wt, dr=True)
        ph.emit("tensor", lambda b, tb=tb, c=c: b.matmul(
            bankS2[0:1, 0:w], onescol128[:, 0:1], tb,
            start=(c == 0), stop=(c == NCH - 1)), waits=["vector"])
        s2_done[c] = ph.snap()["tensor"]
    ph.emit("vector", lambda b: b.tensor_scalar(
        mu, bankS1[0:1, 0:w], 1.0 / D, None, AluOpType.mult), waits=["tensor"])
    ph.emit("vector", lambda b: b.tensor_scalar(
        t, bankS2[0:1, 0:w], 1.0 / D, None, AluOpType.mult))
    ph.emit("vector", lambda b: b.tensor_tensor(var, mu, mu, AluOpType.mult))
    ph.emit("vector", lambda b: b.tensor_tensor(var, t, var, AluOpType.subtract))
    ph.emit("vector", lambda b: b.tensor_scalar(
        var, var, float(EPS), None, AluOpType.add))
    ph.emit("scalar", lambda b: b.activation(t, var, AF.Ln), waits=["vector"])
    ph.emit("vector", lambda b: b.tensor_scalar(
        var, t, -0.5, None, AluOpType.mult), waits=["scalar"])
    ph.emit("scalar", lambda b: b.activation(rstd, var, AF.Exp), waits=["vector"])
    ph.emit("vector", lambda b: b.scalar_tensor_tensor(
        t, mu, -1.0, rstd, AluOpType.mult, AluOpType.mult), waits=["scalar"])
    return ph.snap()["vector"]


def _emit_ln_apply(ph, w, xt, tmp, rows, bankA, bankB, onesrow, ln_out,
                   ln_off, ln_stride, rows_mark=None):
    rstd = rows[0:1, 1536:1536 + w]
    brow = rows[0:1, 512:512 + w]
    wt = {"vector": rows_mark} if rows_mark else ["vector"]
    ph.emit("tensor", lambda b: b.matmul(bankA[:, 0:w], onesrow[0:1, :], rstd,
                                         start=True, stop=True),
            waits=wt)
    ph.emit("tensor", lambda b: b.matmul(bankB[:, 0:w], onesrow[0:1, :], brow,
                                         start=True, stop=True))
    for c in range(NCH):
        xc = xt[:, c * w:(c + 1) * w]
        oc = ln_out[:, c * ln_stride + ln_off: c * ln_stride + ln_off + w]
        tb = tmp[:, (c % 2) * 512:(c % 2) * 512 + w]
        ph.emit("vector", lambda b, xc=xc, tb=tb: b.tensor_tensor(
            tb, xc, bankA[:, 0:w], AluOpType.mult),
            waits=(["tensor"] if c == 0 else None), dr=True)
        ph.emit("vector", lambda b, tb=tb, oc=oc: b.tensor_tensor(
            oc, tb, bankB[:, 0:w], AluOpType.add), dr=True)


def _emit_proj(ph, wsb, rhs_sb, w, banks, epilogue, bank_free_engine,
               rhs_wait=None, prev_free=None):
    if prev_free is None:
        prev_free = {"vector": ph.snap()["vector"],
                     "scalar": ph.snap()["scalar"]}
    free_at = {}
    for o in range(NCH):
        bank = banks[o % 2]
        for c in range(NCH):
            lhs = wsb[:, c * D + o * 128: c * D + o * 128 + 128]
            rc = rhs_sb[:, c * w:(c + 1) * w]
            wt = None
            if c == 0:
                if o < 2:
                    wt = dict(prev_free)
                    if o == 0 and rhs_wait:
                        for k, v in rhs_wait.items():
                            wt[k] = max(wt.get(k, 0), v)
                else:
                    wt = {bank_free_engine: free_at[o - 2]}
            ph.emit("tensor", lambda b, lhs=lhs, rc=rc, c=c, bank=bank, w=w: b.matmul(
                bank[:, 0:w], lhs, rc, start=(c == 0), stop=(c == NCH - 1)),
                waits=wt)
        epilogue(o, bank)
        free_at[o] = ph.snap()[bank_free_engine]
    return free_at


def _ffn_phase(ph, src, dst, tiles, nseg, seg_stride,
               xt, ln1, xr, xk, rti, ka, tmp, rows,
               wr_sb, wk_sb, wv_sb, wrd, wkd, wvd,
               bS1, bS2, bA, bB, bP0, bP1,
               onesrow, onescol128, murs, muks, m8, out_shift=0):
    for c in range(NCH):
        for wsb, wd in ((wr_sb, wrd), (wk_sb, wkd), (wv_sb, wvd)):
            ph.emit("sync", lambda b, wsb=wsb, wd=wd, c=c: b.dma_start(
                out=wsb[:, c * D:(c + 1) * D], in_=wd[c * 128:(c + 1) * 128, :]))
    W = tiles[0][1]
    seq = [(seg, s, w) for seg in range(nseg) for (s, w) in tiles]
    n = len(seq)
    xt_free = {}
    rows_mark = {}
    mix_mark = {}
    ksq_mark = {}

    def xbuf(i):
        return xt[:, (i % 2) * NCH * W:(i % 2) * NCH * W + NCH * W]

    def loads(i):
        seg, st_, w = seq[i]
        col0 = seg * seg_stride + st_
        xb = xbuf(i)
        wt = dict(xt_free.get(i - 2, {}))
        for c in range(NCH):
            ph.emit("sync", lambda b, c=c, col0=col0, w=w, xb=xb: b.dma_start(
                out=xb[:, c * w:(c + 1) * w],
                in_=src[c * 128:(c + 1) * 128, col0:col0 + w]), waits=wt)

    def stats(i):
        seg, st_, w = seq[i]
        rows_mark[i] = _emit_ln_stats(ph, w, xbuf(i), tmp, rows, bS1, bS2,
                                      onescol128)

    def ln2(i):
        seg, st_, w = seq[i]
        xb = xbuf(i)
        if st_ == tiles[0][0]:
            ph.emit("vector", lambda b: b.memset(m8[:, :], 0.0), inc=1, dr=True)
        _emit_ln_apply(ph, w, xb, tmp, rows, bA, bB, onesrow, ln1, 1, W + 1,
                       rows_mark=rows_mark[i])
        for c in range(NCH):
            ph.emit("vector", lambda b, c=c: b.tensor_copy(
                ln1[:, c * (W + 1):c * (W + 1) + 1], m8[:, c:c + 1]), dr=True)
        for c in range(NCH):
            ph.emit("vector", lambda b, c=c, w=w: b.tensor_copy(
                m8[:, c:c + 1], ln1[:, c * (W + 1) + w:c * (W + 1) + w + 1]),
                dr=True)
        mixwt = {"tensor": ph.snap()["tensor"]}
        for c in range(NCH):
            cur = ln1[:, c * (W + 1) + 1:c * (W + 1) + 1 + w]
            prv = ln1[:, c * (W + 1):c * (W + 1) + w]
            dbuf = tmp[:, (c % 2) * 512:(c % 2) * 512 + w]
            ph.emit("vector", lambda b, cur=cur, prv=prv, dbuf=dbuf:
                    b.tensor_tensor(dbuf, cur, prv, AluOpType.subtract),
                    waits=(mixwt if c == 0 else None), dr=True)
            ph.emit("vector", lambda b, dbuf=dbuf, prv=prv, c=c, w=w:
                    b.scalar_tensor_tensor(xr[:, c * w:(c + 1) * w], dbuf,
                                           murs[:, c:c + 1], prv,
                                           AluOpType.mult, AluOpType.add),
                    dr=True)
            ph.emit("vector", lambda b, dbuf=dbuf, prv=prv, c=c, w=w:
                    b.scalar_tensor_tensor(xk[:, c * w:(c + 1) * w], dbuf,
                                           muks[:, c:c + 1], prv,
                                           AluOpType.mult, AluOpType.add),
                    dr=False)
        mix_mark[i] = ph.snap()["vector"]

    def proj_r(i, prev_free):
        seg, st_, w = seq[i]

        def r_epi(o, bank, w=w):
            ph.emit("scalar", lambda b, o=o, bank=bank, w=w: b.activation(
                rti[:, o * w:(o + 1) * w], bank[:, 0:w], AF.Sigmoid),
                waits=["tensor"])
        return _emit_proj(ph, wr_sb, xr, w, (bP0, bP1), r_epi, "scalar",
                          rhs_wait={"vector": mix_mark[i]},
                          prev_free=prev_free)

    def proj_k(i, prev_free):
        seg, st_, w = seq[i]

        def k_epi(o, bank, w=w):
            ph.emit("scalar", lambda b, o=o, bank=bank, w=w: b.activation(
                ka[:, o * w:(o + 1) * w], bank[:, 0:w], AF.Relu),
                waits=["tensor"])
            ph.emit("vector", lambda b, o=o, w=w: b.tensor_tensor(
                ka[:, o * w:(o + 1) * w], ka[:, o * w:(o + 1) * w],
                ka[:, o * w:(o + 1) * w], AluOpType.mult),
                waits=["scalar"], dr=False)
        fr = _emit_proj(ph, wk_sb, xk, w, (bP0, bP1), k_epi, "scalar",
                        rhs_wait={"vector": mix_mark[i]}, prev_free=prev_free)
        ksq_mark[i] = ph.snap()["vector"]
        return fr

    def proj_v(i, prev_free):
        seg, st_, w = seq[i]
        col0 = seg * seg_stride + st_
        xb = xbuf(i)

        def v_epi(o, bank, w=w, col0=col0, xb=xb):
            ph.emit("vector", lambda b, bank=bank, o=o, w=w:
                    b.tensor_tensor(rti[:, o * w:(o + 1) * w],
                                    rti[:, o * w:(o + 1) * w], bank[:, 0:w],
                                    AluOpType.mult),
                    waits=["tensor", "gpsimd"], dr=True)
            ph.emit("vector", lambda b, o=o, w=w, xb=xb:
                    b.scalar_tensor_tensor(xb[:, o * w:(o + 1) * w],
                                           rti[:, o * w:(o + 1) * w], 0.5,
                                           xb[:, o * w:(o + 1) * w],
                                           AluOpType.mult, AluOpType.add),
                    dr=False)
            dst_c0 = col0 - out_shift
            skip = max(0, -dst_c0)
            if w - skip > 0:
                ph.emit("gpsimd", lambda b, o=o, dst_c0=dst_c0, skip=skip,
                        w=w, xb=xb: b.dma_start(
                            out=dst[o * 128:(o + 1) * 128,
                                    dst_c0 + skip:dst_c0 + w],
                            in_=xb[:, o * w + skip:o * w + w]), waits=["vector"])
        fr = _emit_proj(ph, wv_sb, ka, w, (bP0, bP1), v_epi, "vector",
                        rhs_wait={"vector": ksq_mark[i]}, prev_free=prev_free)
        xt_free[i] = {"vector": ph.snap()["vector"],
                      "gpsimd": ph.snap()["gpsimd"]}
        return fr

    loads(0)
    stats(0)
    ln2(0)
    pf = None
    for i in range(n):
        if i + 1 < n:
            loads(i + 1)
        fr = proj_r(i, pf and {"vector": pf})
        if i + 1 < n:
            stats(i + 1)
        fk = proj_k(i, {"scalar": max(fr[NCH - 2], fr[NCH - 1])})
        if i + 1 < n:
            ln2(i + 1)
        fv = proj_v(i, {"scalar": max(fk[NCH - 2], fk[NCH - 1])})
        pf = max(fv[NCH - 2], fv[NCH - 1])
def _sbuf(st, nc, name, shape):
    return st.enter_context(nc.sbuf_tensor(name, shape, F32))


def _psum(st, nc, name, shape):
    return st.enter_context(nc.psum_tensor(name, shape, F32))


def _sems(st, nc, pfx):
    return {e: st.enter_context(nc.semaphore(f"{pfx}_{e}")) for e in ENGINES}


def build_nc():
    nc = bass.Bass(trn_type="TRN2", target_bir_lowering=False)
    scr_kind = "ExternalOutput" if _DEBUG else "Internal"

    xin = nc.dram_tensor("xin", [D, NSEG * SEG], F32, kind="ExternalInput").ap()
    wts = {}
    for nm in ("wqT", "wkT", "wvT", "woT", "wrT", "wkfT", "wvfT"):
        wts[nm] = nc.dram_tensor(nm, [D, D], F32, kind="ExternalInput").ap()
    murd = nc.dram_tensor("mur", [128, NCH], F32, kind="ExternalInput").ap()
    mukd = nc.dram_tensor("muk", [128, NCH], F32, kind="ExternalInput").ap()
    qmaskd = nc.dram_tensor("qmask", [1, NQ], F32, kind="ExternalInput").ap()
    y = nc.dram_tensor("y", [D, 2048], F32, kind="ExternalOutput").ap()

    xres = nc.dram_tensor("xres", [D, NSEG * SEG], F32, kind=scr_kind).ap()
    kscr = nc.dram_tensor("kscr", [D, NKEY], F32, kind=scr_kind).ap()
    vscr = nc.dram_tensor("vscr", [NH, NKEY, HD], F32, kind=scr_kind).ap()
    qscr = nc.dram_tensor("qscr", [D, SEG], F32, kind=scr_kind).ap()
    oscr = nc.dram_tensor("oscr", [D, NQ], F32, kind=scr_kind).ap()
    x2scr = nc.dram_tensor("x2scr", [D, NQ], F32, kind=scr_kind).ap()

    with ExitStack() as st0:
        murs = _sbuf(st0, nc, "murs", [128, NCH])
        muks = _sbuf(st0, nc, "muks", [128, NCH])
        qmask_sb = _sbuf(st0, nc, "qmask_sb", [1, NQ])
        onesrow = _sbuf(st0, nc, "onesrow", [1, 128])
        onescol128 = _sbuf(st0, nc, "onescol128", [128, 1])
        onescol64 = _sbuf(st0, nc, "onescol64", [64, 1])
        ones65 = _sbuf(st0, nc, "ones65", [65, 64])
        m8 = _sbuf(st0, nc, "m8", [128, NCH])

        with ExitStack() as st:
            st_sync = st.enter_context(nc.semaphore("st_sync"))
            with nc.Block("setup") as blk:
                @blk.sync
                def _(b):
                    b.dma_start(out=murs[:, :], in_=murd[:, :]).then_inc(st_sync, 16)
                    b.dma_start(out=muks[:, :], in_=mukd[:, :]).then_inc(st_sync, 16)
                    b.dma_start(out=qmask_sb[0:1, :], in_=qmaskd[:, :]).then_inc(st_sync, 16)
                    b.wait_ge(st_sync, 48)

                @blk.gpsimd
                def _(b):
                    b.memset(onesrow[:, :], 1.0)
                    b.memset(onescol128[:, :], 1.0)
                    b.memset(onescol64[:, :], 1.0)
                    b.memset(ones65[:, :], 1.0)
                    b.memset(m8[:, :], 0.0)

        with ExitStack() as st:
            xt = _sbuf(st, nc, "p1_xt", [128, 2 * NCH * 410])
            ln1 = _sbuf(st, nc, "p1_ln", [128, NCH * 411])
            xr = _sbuf(st, nc, "p1_xr", [128, NCH * 410])
            xk = _sbuf(st, nc, "p1_xk", [128, NCH * 410])
            rti = _sbuf(st, nc, "p1_r", [128, NCH * 410])
            ka = _sbuf(st, nc, "p1_ka", [128, NCH * 410])
            tmp = _sbuf(st, nc, "p1_tmp", [128, 2 * 512])
            rows = _sbuf(st, nc, "p1_rows", [1, 4 * 512])
            wr_sb = _sbuf(st, nc, "p1_wr", [128, NCH * D])
            wk_sb = _sbuf(st, nc, "p1_wk", [128, NCH * D])
            wv_sb = _sbuf(st, nc, "p1_wv", [128, NCH * D])
            bS1 = _psum(st, nc, "p1_bS1", [128, 512])
            bS2 = _psum(st, nc, "p1_bS2", [128, 512])
            bA = _psum(st, nc, "p1_bA", [128, 512])
            bB = _psum(st, nc, "p1_bB", [128, 512])
            bP0 = _psum(st, nc, "p1_bP0", [128, 512])
            bP1 = _psum(st, nc, "p1_bP1", [128, 512])
            sems = _sems(st, nc, "pa")
            with nc.Block("ffn1") as blk:
                ph = Ph()
                _ffn_phase(ph, xin, xres, SEG_TILES, NSEG, SEG,
                           xt, ln1, xr, xk, rti, ka, tmp, rows,
                           wr_sb, wk_sb, wv_sb, wts["wrT"], wts["wkfT"],
                           wts["wvfT"], bS1, bS2, bA, bB, bP0, bP1,
                           onesrow, onescol128, murs, muks, m8)
                ph.realize(nc, blk, sems)

        with ExitStack() as st:
            xt = _sbuf(st, nc, "p2_xt", [128, 2 * NCH * 512])
            ln2 = _sbuf(st, nc, "p2_ln", [128, 2 * NCH * 512])
            kstage = _sbuf(st, nc, "p2_ks", [128, NCH * 512])
            vstage = _sbuf(st, nc, "p2_vt", [128, 2 * D])
            tmp = _sbuf(st, nc, "p2_tmp", [128, 2 * 512])
            rows = _sbuf(st, nc, "p2_rows", [1, 4 * 512])
            wk_sb = _sbuf(st, nc, "p2_wk", [128, NCH * D])
            wv_sb = _sbuf(st, nc, "p2_wv", [128, NCH * D])
            wq_sb = _sbuf(st, nc, "p2_wq", [128, NCH * D])
            bS1 = _psum(st, nc, "p2_bS1", [128, 512])
            bS2 = _psum(st, nc, "p2_bS2", [128, 512])
            bA = _psum(st, nc, "p2_bA", [128, 512])
            bB = _psum(st, nc, "p2_bB", [128, 512])
            bP0 = _psum(st, nc, "p2_bP0", [128, 512])
            bP1 = _psum(st, nc, "p2_bP1", [128, 512])
            sems = _sems(st, nc, "pb")
            with nc.Block("kvq") as blk:
                ph = Ph()
                for c in range(NCH):
                    for wsb, wd in ((wk_sb, wts["wkT"]), (wv_sb, wts["wvT"]),
                                    (wq_sb, wts["wqT"])):
                        ph.emit("sync", lambda b, wsb=wsb, wd=wd, c=c: b.dma_start(
                            out=wsb[:, c * D:(c + 1) * D],
                            in_=wd[c * 128:(c + 1) * 128, :]))
                seq2 = []
                for seg in range(NSEG):
                    for (s_, w) in KEY_TILES:
                        seq2.append(("kv", seg * SEG + s_, w,
                                     seg * 2048 + (s_ - 2)))
                for (s_, w) in Q_TILES:
                    seq2.append(("q", s_, w, s_))
                n2 = len(seq2)
                xt_free = {}

                def xb2(i):
                    return xt[:, (i % 2) * NCH * 512:(i % 2) * NCH * 512 + NCH * 512]

                def lb2(i):
                    return ln2[:, (i % 2) * NCH * 512:(i % 2) * NCH * 512 + NCH * 512]

                rows_mark2 = {}
                ln_mark2 = {}

                def loads2(i):
                    kind, col0, w, aux = seq2[i]
                    xb = xb2(i)
                    wt = dict(xt_free.get(i - 2, {}))
                    for c in range(NCH):
                        ph.emit("sync", lambda b, c=c, col0=col0, w=w, xb=xb:
                                b.dma_start(
                                    out=xb[:, c * w:(c + 1) * w],
                                    in_=xres[c * 128:(c + 1) * 128,
                                             col0:col0 + w]), waits=wt)

                def stats2(i):
                    kind, col0, w, aux = seq2[i]
                    rows_mark2[i] = _emit_ln_stats(ph, w, xb2(i), tmp, rows,
                                                   bS1, bS2, onescol128)

                def lnapply(i):
                    kind, col0, w, aux = seq2[i]
                    _emit_ln_apply(ph, w, xb2(i), tmp, rows, bA, bB, onesrow,
                                   lb2(i), 0, w, rows_mark=rows_mark2[i])
                    xt_free[i] = {"vector": ph.snap()["vector"]}
                    ln_mark2[i] = ph.snap()["vector"]

                def projK(i, prev_free):
                    kind, col0, w, aux = seq2[i]
                    lnb = lb2(i)
                    if kind == "q":
                        def q_epi(o, bank, aux=aux, w=w):
                            oc = kstage[:, o * w:(o + 1) * w]
                            ph.emit("scalar", lambda b, oc=oc, bank=bank, w=w:
                                    b.copy(out=oc, in_=bank[:, 0:w]),
                                    waits=["tensor", "gpsimd"])
                            ph.emit("gpsimd", lambda b, oc=oc, o=o, aux=aux, w=w:
                                    b.dma_start(
                                        out=qscr[o * 128:(o + 1) * 128,
                                                 aux:aux + w], in_=oc),
                                    waits=["scalar"])
                        return _emit_proj(
                            ph, wq_sb, lnb, w, (bP0, bP1), q_epi, "scalar",
                            rhs_wait={"vector": ln_mark2[i]},
                            prev_free=prev_free)
                    keybase = aux

                    def k_epi(o, bank, keybase=keybase, w=w):
                        oc = kstage[:, o * w:(o + 1) * w]
                        ph.emit("scalar", lambda b, oc=oc, bank=bank, w=w:
                                b.copy(out=oc, in_=bank[:, 0:w]),
                                waits=["tensor", "gpsimd"])
                        ph.emit("gpsimd", lambda b, oc=oc, o=o, keybase=keybase,
                                w=w: b.dma_start(
                                    out=kscr[o * 128:(o + 1) * 128,
                                             keybase:keybase + w], in_=oc),
                                waits=["scalar"])
                    return _emit_proj(
                        ph, wk_sb, lnb, w, (bP0, bP1), k_epi, "scalar",
                        rhs_wait={"vector": ln_mark2[i]}, prev_free=prev_free)

                def projV(i, prev_free):
                    kind, col0, w, aux = seq2[i]
                    if kind == "q":
                        return prev_free
                    lnb = lb2(i)
                    keybase = aux
                    last_sc = None
                    for tsub in range(w // 128):
                        vbase = (tsub % 2) * D
                        for dvh in range(2):
                            bank = (bP0, bP1)[dvh]
                            wt = {"scalar": ph.snap()["scalar"]}
                            for c in range(NCH):
                                lhs = lnb[:, c * w + tsub * 128:
                                          c * w + tsub * 128 + 128]
                                rhsw = wv_sb[:, c * D + dvh * 512:
                                             c * D + dvh * 512 + 512]
                                ph.emit("tensor", lambda b, lhs=lhs, rhsw=rhsw,
                                        c=c, bank=bank: b.matmul(
                                            bank[:, :], lhs, rhsw,
                                            start=(c == 0),
                                            stop=(c == NCH - 1)),
                                        waits=(wt if c == 0 else None))
                            oc = vstage[:, vbase + dvh * 512:
                                        vbase + dvh * 512 + 512]
                            ph.emit("scalar", lambda b, oc=oc, bank=bank: b.copy(
                                out=oc, in_=bank[:, :]),
                                waits=["tensor", "gpsimd"])
                        krow = keybase + tsub * 128
                        for hd_ in range(NH):
                            srcv = vstage[:, vbase + hd_ * HD:
                                          vbase + hd_ * HD + HD]
                            ph.emit("gpsimd", lambda b, srcv=srcv, hd_=hd_,
                                    krow=krow: b.dma_start(
                                        out=vscr[hd_, krow:krow + 128, :],
                                        in_=srcv), waits=["scalar"])
                    return {"scalar": ph.snap()["scalar"]}

                loads2(0)
                stats2(0)
                lnapply(0)
                pf2 = None
                for i in range(n2):
                    if i + 1 < n2:
                        loads2(i + 1)
                    fk2 = projK(i, pf2)
                    if i + 1 < n2:
                        stats2(i + 1)
                        lnapply(i + 1)
                    pf2 = projV(i, {"scalar": max(fk2[NCH - 2], fk2[NCH - 1])})
                    if isinstance(pf2, dict) and "scalar" not in pf2:
                        pf2 = None
                ph.realize(nc, blk, sems)

        with ExitStack() as st:
            ktile = _sbuf(st, nc, "a_k", [128, 2 * NKEY])
            qtile = _sbuf(st, nc, "a_q", [128, 2 * SEG])
            vtileA = _sbuf(st, nc, "a_va", [128, 2 * 2048])
            vtileB = _sbuf(st, nc, "a_vb", [128, 2 * 2048])
            ksq = _sbuf(st, nc, "a_ksq", [128, NKEY])
            qsq = _sbuf(st, nc, "a_qsq", [128, SEG])
            krowA = _sbuf(st, nc, "a_krowa", [1, NKEY])
            krowB = _sbuf(st, nc, "a_krowb", [1, NKEY])
            mk2A = _sbuf(st, nc, "a_mk2a", [1, 1])
            mk2B = _sbuf(st, nc, "a_mk2b", [1, 1])
            negcA = _sbuf(st, nc, "a_negca", [1, SEG])
            negcB = _sbuf(st, nc, "a_negcb", [1, SEG])
            cbcAB = _sbuf(st, nc, "a_cbc", [128, 1024])
            ssub = _sbuf(st, nc, "a_ss", [128, 2 * 1024])
            pring = _sbuf(st, nc, "a_pp", [128, 2 * 1024])
            sacc = _sbuf(st, nc, "a_sc", [128, 1024])
            rrowA = _sbuf(st, nc, "a_rra", [1, 512])
            rrowB = _sbuf(st, nc, "a_rrb", [1, 512])
            rbc_sb = _sbuf(st, nc, "a_rbc", [128, 512])
            ostg = _sbuf(st, nc, "a_ostg", [128, 512])
            sAB0 = _psum(st, nc, "a_s01", [128, 1024])
            sAB1 = _psum(st, nc, "a_s23", [128, 1024])
            obank = _psum(st, nc, "a_o", [128, 512])
            bcb = _psum(st, nc, "a_bc", [128, 512])
            csb0 = _psum(st, nc, "a_cs0", [1, 512])
            csb1 = _psum(st, nc, "a_cs1", [1, 512])
            sems = _sems(st, nc, "pc")
            with nc.Block("attn") as blk:
                ph = Ph()
                pair_done = {}
                for hp in range(NH // 2):
                    hA, hB = 2 * hp, 2 * hp + 1
                    par = hp % 2
                    kof, qof, vof = par * NKEY, par * SEG, par * 2048
                    ldq = "sync" if par == 0 else "syncb"
                    wt = {"tensor": pair_done[hp - 2]} if hp >= 2 else None
                    ph.emit(ldq, lambda b, hp=hp, kof=kof: b.dma_start(
                        out=ktile[:, kof:kof + NKEY],
                        in_=kscr[hp * 128:(hp + 1) * 128, :]), waits=wt)
                    ph.emit(ldq, lambda b, hp=hp, qof=qof: b.dma_start(
                        out=qtile[:, qof:qof + SEG],
                        in_=qscr[hp * 128:(hp + 1) * 128, :]))
                    for vt, hh in ((vtileA, hA), (vtileB, hB)):
                        for jq in range(4):
                            def vload(b, vt=vt, hh=hh, jq=jq, vof=vof):
                                dst = vt[:, vof + jq * 512:vof + (jq + 1) * 512] \
                                    .rearrange("p (j d) -> p j d", d=64)
                                srcv = vscr[hh, jq * 1024:(jq + 1) * 1024, :] \
                                    .rearrange("(j p) d -> p j d", p=128)
                                return b.dma_start(out=dst, in_=srcv)
                            ph.emit(ldq, vload)
                    ld_mark = ph.snap()[ldq]
                    wtv = {ldq: ld_mark}
                    if hp >= 1:
                        wtv["tensor"] = pair_done[hp - 1]
                    ph.emit("vector", lambda b, kof=kof: b.tensor_tensor(
                        ksq[:, :], ktile[:, kof:kof + NKEY],
                        ktile[:, kof:kof + NKEY], AluOpType.mult), waits=wtv)
                    ph.emit("vector", lambda b, qof=qof: b.tensor_tensor(
                        qsq[:, :], qtile[:, qof:qof + SEG],
                        qtile[:, qof:qof + SEG], AluOpType.mult))
                    vm = ph.snap()["vector"]
                    evac_done = {}
                    for j in range(8):
                        wt2 = {"vector": vm if j < 1 else evac_done[j - 1]}
                        ph.emit("tensor", lambda b, j=j: b.matmul(
                            csb0[0:1, :], onescol128[0:64, 0:1],
                            ksq[0:64, j * 512:(j + 1) * 512],
                            start=True, stop=True), waits=wt2)
                        ph.emit("tensor", lambda b, j=j: b.matmul(
                            csb1[0:1, :], onescol128[64:128, 0:1],
                            ksq[64:128, j * 512:(j + 1) * 512],
                            start=True, stop=True))
                        ph.emit("vector", lambda b, j=j: b.tensor_copy(
                            krowA[0:1, j * 512:(j + 1) * 512], csb0[0:1, :]),
                            waits=["tensor"])
                        ph.emit("vector", lambda b, j=j: b.tensor_copy(
                            krowB[0:1, j * 512:(j + 1) * 512], csb1[0:1, :]))
                        evac_done[j] = ph.snap()["vector"]
                    ph.emit("vector", lambda b: b.reduce_max(
                        mk2A[0:1, 0:1], krowA[0:1, :], axis=AX.X))
                    ph.emit("vector", lambda b: b.reduce_max(
                        mk2B[0:1, 0:1], krowB[0:1, :], axis=AX.X))
                    ph.emit("vector", lambda b: b.tensor_copy(
                        ksq[:, 0:512], ksq[:, 0:512]))
                    vm2 = ph.snap()["vector"]
                    for j, (qs, qw) in enumerate(QB_LIST):
                        wt2 = {"vector": vm2 if j < 1 else evac_done[8 + j - 1]}
                        ph.emit("tensor", lambda b, qs=qs, qw=qw: b.matmul(
                            csb0[0:1, 0:qw], onescol128[0:64, 0:1],
                            qsq[0:64, qs:qs + qw], start=True, stop=True),
                            waits=wt2)
                        ph.emit("tensor", lambda b, qs=qs, qw=qw: b.matmul(
                            csb1[0:1, 0:qw], onescol128[64:128, 0:1],
                            qsq[64:128, qs:qs + qw], start=True, stop=True))
                        ph.emit("vector", lambda b, qs=qs, qw=qw:
                                b.tensor_scalar(
                                    negcA[0:1, qs:qs + qw], csb0[0:1, 0:qw],
                                    mk2A[0:1, 0:1], -1.0 / 16.0,
                                    AluOpType.add, AluOpType.mult),
                                waits=["tensor"])
                        ph.emit("vector", lambda b, qs=qs, qw=qw:
                                b.tensor_scalar(
                                    negcB[0:1, qs:qs + qw], csb1[0:1, 0:qw],
                                    mk2B[0:1, 0:1], -1.0 / 16.0,
                                    AluOpType.add, AluOpType.mult))
                        evac_done[8 + j] = ph.snap()["vector"]
                    for (qs, qw) in QB_LIST:
                        ph.emit("tensor", lambda b, qs=qs, qw=qw: b.matmul(
                            bcb[:, 0:qw], onesrow[0:1, :], negcA[0:1, qs:qs + qw],
                            start=True, stop=True), waits=["vector", "scalar"])
                        ph.emit("scalar", lambda b, qw=qw: b.copy(
                            out=cbcAB[:, 0:qw], in_=bcb[:, 0:qw]), waits=["tensor"])
                        ph.emit("tensor", lambda b, qs=qs, qw=qw: b.matmul(
                            bcb[:, 0:qw], onesrow[0:1, :], negcB[0:1, qs:qs + qw],
                            start=True, stop=True), waits=["scalar"])
                        ph.emit("scalar", lambda b, qw=qw: b.copy(
                            out=cbcAB[:, 512:512 + qw], in_=bcb[:, 0:qw]),
                            waits=["tensor"])
                        cbc_mark = ph.snap()["scalar"]
                        S_mm, S_sub, S_exp, S_sac = {}, {}, {}, {}
                        S_avA, S_avB = {}, {}
                        prev_norm = ph.snap()["vector"]
                        prev_store = ph.snap()["gpsimd"]

                        def abview(t, slot, qw, stride=1024):
                            return t[:, slot * stride:(slot + 1) * stride] \
                                .rearrange("p (h q) -> p h q", q=512)[:, :, 0:qw]

                        for j in range(32):
                            slot = j % 2
                            bk = (sAB0, sAB1)[slot]
                            wt2 = {"scalar": cbc_mark,
                                   "vector": (S_sub[j - 2] if j >= 2
                                              else prev_norm)}
                            ph.emit("tensor", lambda b, j=j, bk=bk, kof=kof,
                                    qof=qof, qs=qs, qw=qw: b.matmul(
                                        bk[:, 0:qw],
                                        ktile[0:64, kof + j * 128:kof + (j + 1) * 128],
                                        qtile[0:64, qof + qs:qof + qs + qw],
                                        start=True, stop=True,
                                        tile_position=(0, 0)), waits=wt2)
                            ph.emit("tensor", lambda b, j=j, bk=bk, kof=kof,
                                    qof=qof, qs=qs, qw=qw: b.matmul(
                                        bk[:, 512:512 + qw],
                                        ktile[64:128, kof + j * 128:kof + (j + 1) * 128],
                                        qtile[64:128, qof + qs:qof + qs + qw],
                                        start=True, stop=True,
                                        tile_position=(64, 0)))
                            S_mm[j] = ph.snap()["tensor"]
                            wt3 = {"tensor": S_mm[j]}
                            if j >= 2:
                                wt3["scalar"] = S_exp[j - 2]
                            ph.emit("vector", lambda b, j=j, bk=bk, qw=qw:
                                    b.scalar_tensor_tensor(
                                        abview(ssub, 2 * (j % 2) + 0, qw, 512)
                                        if False else
                                        ssub[:, (j % 2) * 1024:(j % 2 + 1) * 1024]
                                        .rearrange("p (h q) -> p h q", q=512)[:, :, 0:qw],
                                        bk[:, 0:1024]
                                        .rearrange("p (h q) -> p h q", q=512)[:, :, 0:qw],
                                        0.125,
                                        cbcAB[:, 0:1024]
                                        .rearrange("p (h q) -> p h q", q=512)[:, :, 0:qw],
                                        AluOpType.mult, AluOpType.add),
                                    waits=wt3, dr=False)
                            S_sub[j] = ph.snap()["vector"]
                            wt4 = {"vector": S_sub[j]}
                            if j >= 2:
                                wt4["tensor"] = S_avB[j - 2]
                            ph.emit("scalar", lambda b, j=j, qw=qw: b.activation(
                                pring[:, (j % 2) * 1024:(j % 2 + 1) * 1024]
                                .rearrange("p (h q) -> p h q", q=512)[:, :, 0:qw],
                                ssub[:, (j % 2) * 1024:(j % 2 + 1) * 1024]
                                .rearrange("p (h q) -> p h q", q=512)[:, :, 0:qw],
                                AF.Exp), waits=wt4, dr=False)
                            S_exp[j] = ph.snap()["scalar"]
                            if j == 0:
                                ph.emit("vector", lambda b, qw=qw: b.tensor_copy(
                                    sacc[:, 0:1024]
                                    .rearrange("p (h q) -> p h q", q=512)[:, :, 0:qw],
                                    pring[:, 0:1024]
                                    .rearrange("p (h q) -> p h q", q=512)[:, :, 0:qw]),
                                    waits={"scalar": S_exp[0],
                                           "vector": prev_norm}, dr=False)
                            else:
                                ph.emit("vector", lambda b, j=j, qw=qw:
                                        b.tensor_tensor(
                                            sacc[:, 0:1024]
                                            .rearrange("p (h q) -> p h q", q=512)[:, :, 0:qw],
                                            sacc[:, 0:1024]
                                            .rearrange("p (h q) -> p h q", q=512)[:, :, 0:qw],
                                            pring[:, (j % 2) * 1024:(j % 2 + 1) * 1024]
                                            .rearrange("p (h q) -> p h q", q=512)[:, :, 0:qw],
                                            AluOpType.add),
                                        waits={"scalar": S_exp[j],
                                               "vector": S_sac[j - 1]}, dr=False)
                            S_sac[j] = ph.snap()["vector"]
                            if j >= 1:
                                jj = j - 1
                                wt5 = {"scalar": S_exp[jj]}
                                if jj == 0:
                                    wt5["vector"] = prev_norm
                                ph.emit("tensor", lambda b, jj=jj, vof=vof, qw=qw:
                                        b.matmul(
                                            obank[0:64, 0:qw],
                                            vtileA[:, vof + jj * 64:vof + (jj + 1) * 64],
                                            pring[:, (jj % 2) * 1024:(jj % 2) * 1024 + qw],
                                            start=(jj == 0), stop=False,
                                            tile_position=(0, 0),
                                            skip_group_check=True), waits=wt5)
                                S_avA[jj] = ph.snap()["tensor"]
                                ph.emit("tensor", lambda b, jj=jj, vof=vof, qw=qw:
                                        b.matmul(
                                            obank[64:128, 0:qw],
                                            vtileB[:, vof + jj * 64:vof + (jj + 1) * 64],
                                            pring[:, (jj % 2) * 1024 + 512:
                                                  (jj % 2) * 1024 + 512 + qw],
                                            start=(jj == 0), stop=False,
                                            tile_position=(0, 64),
                                            skip_group_check=True))
                                S_avB[jj] = ph.snap()["tensor"]
                        for jj in (31,):
                            ph.emit("tensor", lambda b, jj=jj, vof=vof, qw=qw:
                                    b.matmul(
                                        obank[0:64, 0:qw],
                                        vtileA[:, vof + jj * 64:vof + (jj + 1) * 64],
                                        pring[:, (jj % 2) * 1024:(jj % 2) * 1024 + qw],
                                        start=False, stop=True,
                                        tile_position=(0, 0),
                                        skip_group_check=True),
                                    waits={"scalar": S_exp[jj]})
                            S_avA[jj] = ph.snap()["tensor"]
                            ph.emit("tensor", lambda b, jj=jj, vof=vof, qw=qw:
                                    b.matmul(
                                        obank[64:128, 0:qw],
                                        vtileB[:, vof + jj * 64:vof + (jj + 1) * 64],
                                        pring[:, (jj % 2) * 1024 + 512:
                                              (jj % 2) * 1024 + 512 + qw],
                                        start=False, stop=True,
                                        tile_position=(0, 64),
                                        skip_group_check=True))
                            S_avB[jj] = ph.snap()["tensor"]
                        ph.emit("tensor", lambda b, qw=qw: b.matmul(
                            csb0[0:1, 0:qw], onescol128[:, 0:1], sacc[:, 0:qw],
                            start=True, stop=True), waits=["vector"])
                        ph.emit("tensor", lambda b, qw=qw: b.matmul(
                            csb1[0:1, 0:qw], onescol128[:, 0:1],
                            sacc[:, 512:512 + qw],
                            start=True, stop=True))
                        ph.emit("vector", lambda b, qw=qw: b.reciprocal(
                            rrowA[0:1, 0:qw], csb0[0:1, 0:qw]), waits=["tensor"])
                        ph.emit("vector", lambda b, qs=qs, qw=qw: b.tensor_tensor(
                            rrowA[0:1, 0:qw], rrowA[0:1, 0:qw],
                            qmask_sb[0:1, qs - 1:qs - 1 + qw], AluOpType.mult))
                        ph.emit("vector", lambda b, qw=qw: b.reciprocal(
                            rrowB[0:1, 0:qw], csb1[0:1, 0:qw]))
                        ph.emit("vector", lambda b, qs=qs, qw=qw: b.tensor_tensor(
                            rrowB[0:1, 0:qw], rrowB[0:1, 0:qw],
                            qmask_sb[0:1, qs - 1:qs - 1 + qw], AluOpType.mult))
                        ph.emit("tensor", lambda b, qw=qw: b.matmul(
                            bcb[0:64, 0:qw], onesrow[0:1, 0:64], rrowA[0:1, 0:qw],
                            start=True, stop=True, tile_position=(0, 0),
                            skip_group_check=True), waits=["vector", "scalar"])
                        ph.emit("tensor", lambda b, qw=qw: b.matmul(
                            bcb[64:128, 0:qw], onesrow[0:1, 0:64], rrowB[0:1, 0:qw],
                            start=True, stop=True, tile_position=(0, 64),
                            skip_group_check=True))
                        ph.emit("scalar", lambda b, qw=qw: b.copy(
                            out=rbc_sb[:, 0:qw], in_=bcb[:, 0:qw]),
                            waits=["tensor"])
                        ph.emit("vector", lambda b, qw=qw: b.tensor_tensor(
                            ostg[:, 0:qw], obank[:, 0:qw], rbc_sb[:, 0:qw],
                            AluOpType.mult),
                            waits={"scalar": ph.snap()["scalar"],
                                   "gpsimd": prev_store})
                        ph.emit("gpsimd", lambda b, hp=hp, qs=qs, qw=qw: b.dma_start(
                            out=oscr[hp * 128:(hp + 1) * 128, qs - 1:qs - 1 + qw],
                            in_=ostg[:, 0:qw]), waits=["vector"])
                    pair_done[hp] = ph.snap()["tensor"]
                ph.realize(nc, blk, sems)

        with ExitStack() as st:
            otile = _sbuf(st, nc, "c_o", [128, NCH * 512])
            xtile = _sbuf(st, nc, "c_x", [128, NCH * 512])
            x2tile = _sbuf(st, nc, "c_x2", [128, NCH * 512])
            wo_sb = _sbuf(st, nc, "c_wo", [128, NCH * D])
            bP0 = _psum(st, nc, "c_bP0", [128, 512])
            bP1 = _psum(st, nc, "c_bP1", [128, 512])
            sems = _sems(st, nc, "pd")
            with nc.Block("wo") as blk:
                ph = Ph()
                for c in range(NCH):
                    ph.emit("sync", lambda b, c=c: b.dma_start(
                        out=wo_sb[:, c * D:(c + 1) * D],
                        in_=wts["woT"][c * 128:(c + 1) * 128, :]))
                for (s, w) in F3_TILES:
                    ld_prev = ph.snap()
                    for c in range(NCH):
                        ph.emit("sync", lambda b, c=c, s=s, w=w: b.dma_start(
                            out=otile[:, c * w:(c + 1) * w],
                            in_=oscr[c * 128:(c + 1) * 128, s:s + w]),
                            waits={"tensor": ld_prev["tensor"]})
                        ph.emit("sync", lambda b, c=c, s=s, w=w: b.dma_start(
                            out=xtile[:, c * w:(c + 1) * w],
                            in_=xres[c * 128:(c + 1) * 128, 1 + s:1 + s + w]),
                            waits={"vector": ld_prev["vector"]})

                    def wo_epi(o, bank, s=s, w=w):
                        ph.emit("vector", lambda b, o=o, bank=bank, w=w:
                                b.tensor_tensor(
                                    x2tile[:, o * w:(o + 1) * w], bank[:, 0:w],
                                    xtile[:, o * w:(o + 1) * w], AluOpType.add),
                                waits=["tensor", "gpsimd"])
                        ph.emit("gpsimd", lambda b, o=o, s=s, w=w: b.dma_start(
                            out=x2scr[o * 128:(o + 1) * 128, s:s + w],
                            in_=x2tile[:, o * w:(o + 1) * w]), waits=["vector"])
                    ph.emit("vector", lambda b: b.memset(x2tile[0:1, 0:1], 0.0),
                            waits=["sync"])
                    _emit_proj(ph, wo_sb, otile, w, (bP0, bP1), wo_epi, "vector")
                ph.realize(nc, blk, sems)

        with ExitStack() as st:
            xt = _sbuf(st, nc, "d_xt", [128, 2 * NCH * 410])
            ln1 = _sbuf(st, nc, "d_ln", [128, NCH * 411])
            xr = _sbuf(st, nc, "d_xr", [128, NCH * 410])
            xk = _sbuf(st, nc, "d_xk", [128, NCH * 410])
            rti = _sbuf(st, nc, "d_r", [128, NCH * 410])
            ka = _sbuf(st, nc, "d_ka", [128, NCH * 410])
            tmp = _sbuf(st, nc, "d_tmp", [128, 2 * 512])
            rows = _sbuf(st, nc, "d_rows", [1, 4 * 512])
            wr_sb = _sbuf(st, nc, "d_wr", [128, NCH * D])
            wk_sb = _sbuf(st, nc, "d_wk", [128, NCH * D])
            wv_sb = _sbuf(st, nc, "d_wv", [128, NCH * D])
            bS1 = _psum(st, nc, "d_bS1", [128, 512])
            bS2 = _psum(st, nc, "d_bS2", [128, 512])
            bA = _psum(st, nc, "d_bA", [128, 512])
            bB = _psum(st, nc, "d_bB", [128, 512])
            bP0 = _psum(st, nc, "d_bP0", [128, 512])
            bP1 = _psum(st, nc, "d_bP1", [128, 512])
            sems = _sems(st, nc, "pe")
            with nc.Block("ffn3") as blk:
                ph = Ph()
                _ffn_phase(ph, x2scr, y, F3_TILES, 1, NQ,
                           xt, ln1, xr, xk, rti, ka, tmp, rows,
                           wr_sb, wk_sb, wv_sb, wts["wrT"], wts["wkfT"],
                           wts["wvfT"], bS1, bS2, bA, bB, bP0, bP1,
                           onesrow, onescol128, murs, muks, m8, out_shift=1)
                ph.realize(nc, blk, sems)

    return nc


def _host_prepare(inputs):
    x = np.asarray(inputs["L0_attn_x"], dtype=np.float32)
    wT = {}
    for nm, key in (("wqT", "L0_attn_Wq"), ("wkT", "L0_attn_Wk"),
                    ("wvT", "L0_attn_Wv"), ("woT", "L0_attn_Wo"),
                    ("wrT", "L0_ffn_W_r"), ("wkfT", "L0_ffn_W_k"),
                    ("wvfT", "L0_ffn_W_v")):
        wT[nm] = np.ascontiguousarray(np.asarray(inputs[key], np.float32).T)
    mur = np.ascontiguousarray(
        np.asarray(inputs["L0_ffn_mu_r"], np.float32).reshape(NCH, 128).T)
    muk = np.ascontiguousarray(
        np.asarray(inputs["L0_ffn_mu_k"], np.float32).reshape(NCH, 128).T)
    in_maps = []
    for core in range(8):
        b, h = core // 2, core % 2
        xf = np.ascontiguousarray(x[b].T)
        half = [xf[:, 0:2048], xf[:, 2048:4096]]
        ctx = [np.zeros((D, 2), np.float32), np.ascontiguousarray(xf[:, 2046:2048])]
        own, oth = h, 1 - h
        segA = np.concatenate([ctx[own], half[own]], axis=1)
        segB = np.concatenate([ctx[oth], half[oth]], axis=1)
        xin = np.ascontiguousarray(np.concatenate([segA, segB], axis=1))
        qmask = np.ones((1, NQ), np.float32)
        if h == 0:
            qmask[0, 0] = 0.0
        im = {"xin": xin, "mur": mur, "muk": muk, "qmask": qmask}
        im.update(wT)
        in_maps.append(im)
    return in_maps


def kernel(**inputs):
    in_maps = _host_prepare(inputs)
    nc = build_nc()
    res = run_bass_kernel_spmd(nc, in_maps, core_ids=list(range(8)))
    out = np.empty((4, 4096, 1024), np.float32)
    for core in range(8):
        b, h = core // 2, core % 2
        out[b, 2048 * h:2048 * (h + 1), :] = res.results[core]["y"].T
    return out


if __name__ == "__main__":
    import time
    t0 = time.time()
    nc = build_nc()
    print(f"built ok in {time.time() - t0:.1f}s")


# version 17
# speedup vs baseline: 1.1861x; 1.1861x over previous
from contextlib import ExitStack

import numpy as np
import concourse.bass as bass
import concourse.mybir as mybir
from concourse.alu_op_type import AluOpType
from concourse.bass_utils import run_bass_kernel_spmd

F32 = mybir.dt.float32
AF = mybir.ActivationFunctionType
AX = mybir.AxisListType

D = 1024
NCH = 8
SEG = 2050
NSEG = 2
NKEY = 4096
NQ = 2049
NH = 16
HD = 64
EPS = 1e-5
_DEBUG = False

SEG_TILES = [(0, 410), (410, 410), (820, 410), (1230, 410), (1640, 410)]
KEY_TILES = [(2, 512), (514, 512), (1026, 512), (1538, 512)]
Q_TILES = [(0, 410), (410, 410), (820, 410), (1230, 410), (1640, 410)]
QB_LIST = [(1, 410), (411, 410), (821, 410), (1231, 410), (1641, 409)]
F3_TILES = [(0, 410), (410, 410), (820, 410), (1230, 410), (1640, 409)]

ENGINES = ("tensor", "vector", "scalar", "sync", "syncb", "gpsimd")
ENGINE_OF = {"syncb": "sync"}


class Ph:

    def __init__(self):
        self.steps = {e: [] for e in ENGINES}
        self.cnt = {e: 0 for e in ENGINES}
        self.seq = 0

    def snap(self):
        return dict(self.cnt)

    def emit(self, eng, fn, waits=None, inc=None, dr=True):
        if inc is None:
            inc = 16 if eng in ("sync", "syncb", "gpsimd") else 1
        if isinstance(waits, (list, tuple)):
            waits = {e: self.cnt[e] for e in waits if e != eng}
        waits = {e: v for e, v in (waits or {}).items() if v > 0}
        self.cnt[eng] += inc
        self.seq += 1
        self.steps[eng].append((fn, waits, self.cnt[eng], self.seq, dr))

    def realize(self, nc, block, sems):
        waited = {e: set() for e in ENGINES}
        for e in ENGINES:
            for _, waits, _, _, _ in self.steps[e]:
                for we, wv in waits.items():
                    waited[we].add(wv)
        for e in ("sync", "syncb", "gpsimd"):
            if self.cnt[e] > 0:
                waited[e].add(self.cnt[e])
        plans = {}
        for e in ENGINES:
            if e in ("sync", "syncb", "gpsimd"):
                plan = {}
                prev = 0
                for i, (_, _, after, _, _) in enumerate(self.steps[e]):
                    plan[i] = after - prev
                    prev = after
                plans[e] = plan
                continue
            if getattr(self, "dense", False):
                plan = {}
                prev = 0
                for i, (_, _, after, _, _) in enumerate(self.steps[e]):
                    plan[i] = after - prev
                    prev = after
                plans[e] = plan
                continue
            targets = sorted(waited[e])
            plan = {}
            prev = 0
            ti = 0
            for i, (_, _, after, _, _) in enumerate(self.steps[e]):
                amt = 0
                while ti < len(targets) and targets[ti] <= after:
                    amt += targets[ti] - prev
                    prev = targets[ti]
                    ti += 1
                if amt:
                    plan[i] = amt
            plans[e] = plan

        def body(phys):
            merged = []
            for e in ENGINES:
                if ENGINE_OF.get(e, e) == phys:
                    for i, step in enumerate(self.steps[e]):
                        merged.append((step[3], e, i, step))
            merged.sort()

            def run(builder):
                for _, e, i, (fn, waits, _, _, dr) in merged:
                    for we, wv in waits.items():
                        builder.wait_ge(sems[we], wv)
                    inst = fn(builder)
                    if i in plans[e]:
                        assert inst is not None, f"{e} step {i} returned None"
                        inst.then_inc(sems[e], plans[e][i])
                    if dr and phys in ("vector", "scalar"):
                        builder.drain()
                for e in ENGINES:
                    if ENGINE_OF.get(e, e) == phys and \
                            e in ("sync", "syncb", "gpsimd") and self.cnt[e] > 0:
                        builder.wait_ge(sems[e], self.cnt[e])
            return run

        for phys in ("tensor", "vector", "scalar", "sync", "gpsimd"):
            if any(self.steps[e] and ENGINE_OF.get(e, e) == phys
                   for e in ENGINES):
                getattr(block, phys)(body(phys))


def _emit_ln_stats(ph, w, xt, tmp, rows, bankS1, bankS2, onescol128,
                   first_vec_waits=None, ldq="sync"):
    mu = rows[0:1, 0:w]
    t = rows[0:1, 512:512 + w]
    var = rows[0:1, 1024:1024 + w]
    rstd = rows[0:1, 1536:1536 + w]
    for c in range(NCH):
        xc = xt[:, c * w:(c + 1) * w]
        ph.emit("tensor", lambda b, xc=xc, c=c: b.matmul(
            bankS1[0:1, 0:w], onescol128[:, 0:1], xc,
            start=(c == 0), stop=(c == NCH - 1)),
            waits=(["sync", ldq, "vector"] if c == 0 else None))
    s2_done = {}
    for c in range(NCH):
        xc = xt[:, c * w:(c + 1) * w]
        tb = tmp[:, (c % 2) * 512:(c % 2) * 512 + w]
        if c == 0:
            wt = dict(first_vec_waits or {})
            wt["sync"] = ph.snap()["sync"]
            wt[ldq] = ph.snap()[ldq]
        elif c == 1:
            wt = None
        else:
            wt = {"tensor": s2_done[c - 2]}
        ph.emit("vector", lambda b, xc=xc, tb=tb: b.tensor_tensor(
            tb, xc, xc, AluOpType.mult), waits=wt, dr=True)
        ph.emit("tensor", lambda b, tb=tb, c=c: b.matmul(
            bankS2[0:1, 0:w], onescol128[:, 0:1], tb,
            start=(c == 0), stop=(c == NCH - 1)), waits=["vector"])
        s2_done[c] = ph.snap()["tensor"]
    ph.emit("vector", lambda b: b.tensor_scalar(
        mu, bankS1[0:1, 0:w], 1.0 / D, None, AluOpType.mult), waits=["tensor"])
    ph.emit("vector", lambda b: b.tensor_scalar(
        t, bankS2[0:1, 0:w], 1.0 / D, None, AluOpType.mult))
    ph.emit("vector", lambda b: b.tensor_tensor(var, mu, mu, AluOpType.mult))
    ph.emit("vector", lambda b: b.tensor_tensor(var, t, var, AluOpType.subtract))
    ph.emit("vector", lambda b: b.tensor_scalar(
        var, var, float(EPS), None, AluOpType.add))
    ph.emit("scalar", lambda b: b.activation(t, var, AF.Ln), waits=["vector"])
    ph.emit("vector", lambda b: b.tensor_scalar(
        var, t, -0.5, None, AluOpType.mult), waits=["scalar"])
    ph.emit("scalar", lambda b: b.activation(rstd, var, AF.Exp), waits=["vector"])
    ph.emit("vector", lambda b: b.scalar_tensor_tensor(
        t, mu, -1.0, rstd, AluOpType.mult, AluOpType.mult), waits=["scalar"])
    return ph.snap()["vector"]


def _emit_ln_apply(ph, w, xt, tmp, rows, bankA, bankB, onesrow, ln_out,
                   ln_off, ln_stride, rows_mark=None):
    rstd = rows[0:1, 1536:1536 + w]
    brow = rows[0:1, 512:512 + w]
    wt = {"vector": rows_mark} if rows_mark else ["vector"]
    ph.emit("tensor", lambda b: b.matmul(bankA[:, 0:w], onesrow[0:1, :], rstd,
                                         start=True, stop=True),
            waits=wt)
    ph.emit("tensor", lambda b: b.matmul(bankB[:, 0:w], onesrow[0:1, :], brow,
                                         start=True, stop=True))
    for c in range(NCH):
        xc = xt[:, c * w:(c + 1) * w]
        oc = ln_out[:, c * ln_stride + ln_off: c * ln_stride + ln_off + w]
        tb = tmp[:, (c % 2) * 512:(c % 2) * 512 + w]
        ph.emit("vector", lambda b, xc=xc, tb=tb: b.tensor_tensor(
            tb, xc, bankA[:, 0:w], AluOpType.mult),
            waits=(["tensor"] if c == 0 else None), dr=True)
        ph.emit("vector", lambda b, tb=tb, oc=oc: b.tensor_tensor(
            oc, tb, bankB[:, 0:w], AluOpType.add), dr=True)


def _emit_proj(ph, wsb, rhs_sb, w, banks, epilogue, bank_free_engine,
               rhs_wait=None, prev_free=None):
    if prev_free is None:
        prev_free = {"vector": ph.snap()["vector"],
                     "scalar": ph.snap()["scalar"]}
    free_at = {}
    for o in range(NCH):
        bank = banks[o % 2]
        for c in range(NCH):
            lhs = wsb[:, c * D + o * 128: c * D + o * 128 + 128]
            rc = rhs_sb[:, c * w:(c + 1) * w]
            wt = None
            if c == 0:
                if o < 2:
                    wt = dict(prev_free)
                    if o == 0 and rhs_wait:
                        for k, v in rhs_wait.items():
                            wt[k] = max(wt.get(k, 0), v)
                else:
                    wt = {bank_free_engine: free_at[o - 2]}
            ph.emit("tensor", lambda b, lhs=lhs, rc=rc, c=c, bank=bank, w=w: b.matmul(
                bank[:, 0:w], lhs, rc, start=(c == 0), stop=(c == NCH - 1)),
                waits=wt)
        epilogue(o, bank)
        free_at[o] = ph.snap()[bank_free_engine]
    return free_at


def _ffn_phase(ph, src, dst, tiles, nseg, seg_stride,
               xt, ln1, xr, xk, rti, ka, tmp, rows,
               wr_sb, wk_sb, wv_sb, wrd, wkd, wvd,
               bS1, bS2, bA, bB, bP0, bP1,
               onesrow, onescol128, murs, muks, m8, out_shift=0):
    for c in range(NCH):
        for wsb, wd in ((wr_sb, wrd), (wk_sb, wkd), (wv_sb, wvd)):
            ph.emit("sync", lambda b, wsb=wsb, wd=wd, c=c: b.dma_start(
                out=wsb[:, c * D:(c + 1) * D], in_=wd[c * 128:(c + 1) * 128, :]))
    W = tiles[0][1]
    seq = [(seg, s, w) for seg in range(nseg) for (s, w) in tiles]
    n = len(seq)
    xt_free = {}
    rows_mark = {}
    mix_mark = {}
    ksq_mark = {}

    def xbuf(i):
        return xt[:, (i % 2) * NCH * W:(i % 2) * NCH * W + NCH * W]

    def loads(i):
        seg, st_, w = seq[i]
        col0 = seg * seg_stride + st_
        xb = xbuf(i)
        wt = dict(xt_free.get(i - 2, {}))
        q = "sync" if i % 2 == 0 else "syncb"
        for c in range(NCH):
            ph.emit(q, lambda b, c=c, col0=col0, w=w, xb=xb: b.dma_start(
                out=xb[:, c * w:(c + 1) * w],
                in_=src[c * 128:(c + 1) * 128, col0:col0 + w]), waits=wt)

    def stats(i):
        seg, st_, w = seq[i]
        rows_mark[i] = _emit_ln_stats(ph, w, xbuf(i), tmp, rows, bS1, bS2,
                                      onescol128,
                                      ldq=("sync" if i % 2 == 0 else "syncb"))

    def ln2(i):
        seg, st_, w = seq[i]
        xb = xbuf(i)
        if st_ == tiles[0][0]:
            ph.emit("vector", lambda b: b.memset(m8[:, :], 0.0), inc=1, dr=True)
        _emit_ln_apply(ph, w, xb, tmp, rows, bA, bB, onesrow, ln1, 1, W + 1,
                       rows_mark=rows_mark[i])
        for c in range(NCH):
            ph.emit("vector", lambda b, c=c: b.tensor_copy(
                ln1[:, c * (W + 1):c * (W + 1) + 1], m8[:, c:c + 1]), dr=True)
        for c in range(NCH):
            ph.emit("vector", lambda b, c=c, w=w: b.tensor_copy(
                m8[:, c:c + 1], ln1[:, c * (W + 1) + w:c * (W + 1) + w + 1]),
                dr=True)
        mixwt = {"tensor": ph.snap()["tensor"]}
        for c in range(NCH):
            cur = ln1[:, c * (W + 1) + 1:c * (W + 1) + 1 + w]
            prv = ln1[:, c * (W + 1):c * (W + 1) + w]
            dbuf = tmp[:, (c % 2) * 512:(c % 2) * 512 + w]
            ph.emit("vector", lambda b, cur=cur, prv=prv, dbuf=dbuf:
                    b.tensor_tensor(dbuf, cur, prv, AluOpType.subtract),
                    waits=(mixwt if c == 0 else None), dr=True)
            ph.emit("vector", lambda b, dbuf=dbuf, prv=prv, c=c, w=w:
                    b.scalar_tensor_tensor(xr[:, c * w:(c + 1) * w], dbuf,
                                           murs[:, c:c + 1], prv,
                                           AluOpType.mult, AluOpType.add),
                    dr=True)
            ph.emit("vector", lambda b, dbuf=dbuf, prv=prv, c=c, w=w:
                    b.scalar_tensor_tensor(xk[:, c * w:(c + 1) * w], dbuf,
                                           muks[:, c:c + 1], prv,
                                           AluOpType.mult, AluOpType.add),
                    dr=False)
        mix_mark[i] = ph.snap()["vector"]

    def proj_r(i, prev_free):
        seg, st_, w = seq[i]

        def r_epi(o, bank, w=w):
            ph.emit("scalar", lambda b, o=o, bank=bank, w=w: b.activation(
                rti[:, o * w:(o + 1) * w], bank[:, 0:w], AF.Sigmoid),
                waits=["tensor"])
        return _emit_proj(ph, wr_sb, xr, w, (bP0, bP1), r_epi, "scalar",
                          rhs_wait={"vector": mix_mark[i]},
                          prev_free=prev_free)

    def proj_k(i, prev_free):
        seg, st_, w = seq[i]

        def k_epi(o, bank, w=w):
            ph.emit("scalar", lambda b, o=o, bank=bank, w=w: b.activation(
                ka[:, o * w:(o + 1) * w], bank[:, 0:w], AF.Relu),
                waits=["tensor"])
            ph.emit("vector", lambda b, o=o, w=w: b.tensor_tensor(
                ka[:, o * w:(o + 1) * w], ka[:, o * w:(o + 1) * w],
                ka[:, o * w:(o + 1) * w], AluOpType.mult),
                waits=["scalar"], dr=False)
        fr = _emit_proj(ph, wk_sb, xk, w, (bP0, bP1), k_epi, "scalar",
                        rhs_wait={"vector": mix_mark[i]}, prev_free=prev_free)
        ksq_mark[i] = ph.snap()["vector"]
        return fr

    def proj_v(i, prev_free):
        seg, st_, w = seq[i]
        col0 = seg * seg_stride + st_
        xb = xbuf(i)

        def v_epi(o, bank, w=w, col0=col0, xb=xb):
            ph.emit("vector", lambda b, bank=bank, o=o, w=w:
                    b.tensor_tensor(rti[:, o * w:(o + 1) * w],
                                    rti[:, o * w:(o + 1) * w], bank[:, 0:w],
                                    AluOpType.mult),
                    waits=["tensor", "gpsimd"], dr=True)
            ph.emit("vector", lambda b, o=o, w=w, xb=xb:
                    b.scalar_tensor_tensor(xb[:, o * w:(o + 1) * w],
                                           rti[:, o * w:(o + 1) * w], 0.5,
                                           xb[:, o * w:(o + 1) * w],
                                           AluOpType.mult, AluOpType.add),
                    dr=False)
            dst_c0 = col0 - out_shift
            skip = max(0, -dst_c0)
            if w - skip > 0:
                ph.emit("gpsimd", lambda b, o=o, dst_c0=dst_c0, skip=skip,
                        w=w, xb=xb: b.dma_start(
                            out=dst[o * 128:(o + 1) * 128,
                                    dst_c0 + skip:dst_c0 + w],
                            in_=xb[:, o * w + skip:o * w + w]), waits=["vector"])
        fr = _emit_proj(ph, wv_sb, ka, w, (bP0, bP1), v_epi, "vector",
                        rhs_wait={"vector": ksq_mark[i]}, prev_free=prev_free)
        xt_free[i] = {"vector": ph.snap()["vector"],
                      "gpsimd": ph.snap()["gpsimd"]}
        return fr

    loads(0)
    stats(0)
    ln2(0)
    pf = None
    for i in range(n):
        if i + 1 < n:
            loads(i + 1)
        fr = proj_r(i, pf and {"vector": pf})
        if i + 1 < n:
            stats(i + 1)
        fk = proj_k(i, {"scalar": max(fr[NCH - 2], fr[NCH - 1])})
        if i + 1 < n:
            ln2(i + 1)
        fv = proj_v(i, {"scalar": max(fk[NCH - 2], fk[NCH - 1])})
        pf = max(fv[NCH - 2], fv[NCH - 1])
def _sbuf(st, nc, name, shape):
    return st.enter_context(nc.sbuf_tensor(name, shape, F32))


def _psum(st, nc, name, shape):
    return st.enter_context(nc.psum_tensor(name, shape, F32))


def _sems(st, nc, pfx):
    return {e: st.enter_context(nc.semaphore(f"{pfx}_{e}")) for e in ENGINES}


def build_nc():
    nc = bass.Bass(trn_type="TRN2", target_bir_lowering=False)
    scr_kind = "ExternalOutput" if _DEBUG else "Internal"

    xin = nc.dram_tensor("xin", [D, NSEG * SEG], F32, kind="ExternalInput").ap()
    wts = {}
    for nm in ("wqT", "wkT", "wvT", "woT", "wrT", "wkfT", "wvfT"):
        wts[nm] = nc.dram_tensor(nm, [D, D], F32, kind="ExternalInput").ap()
    murd = nc.dram_tensor("mur", [128, NCH], F32, kind="ExternalInput").ap()
    mukd = nc.dram_tensor("muk", [128, NCH], F32, kind="ExternalInput").ap()
    qmaskd = nc.dram_tensor("qmask", [1, NQ], F32, kind="ExternalInput").ap()
    y = nc.dram_tensor("y", [D, 2048], F32, kind="ExternalOutput").ap()

    xres = nc.dram_tensor("xres", [D, NSEG * SEG], F32, kind=scr_kind).ap()
    kscr = nc.dram_tensor("kscr", [D, NKEY], F32, kind=scr_kind).ap()
    vscr = nc.dram_tensor("vscr", [NH, NKEY, HD], F32, kind=scr_kind).ap()
    qscr = nc.dram_tensor("qscr", [D, SEG], F32, kind=scr_kind).ap()
    oscr = nc.dram_tensor("oscr", [D, NQ], F32, kind=scr_kind).ap()
    x2scr = nc.dram_tensor("x2scr", [D, NQ], F32, kind=scr_kind).ap()

    with ExitStack() as st0:
        murs = _sbuf(st0, nc, "murs", [128, NCH])
        muks = _sbuf(st0, nc, "muks", [128, NCH])
        qmask_sb = _sbuf(st0, nc, "qmask_sb", [1, NQ])
        onesrow = _sbuf(st0, nc, "onesrow", [1, 128])
        onescol128 = _sbuf(st0, nc, "onescol128", [128, 1])
        onescol64 = _sbuf(st0, nc, "onescol64", [64, 1])
        ones65 = _sbuf(st0, nc, "ones65", [65, 64])
        m8 = _sbuf(st0, nc, "m8", [128, NCH])

        with ExitStack() as st:
            st_sync = st.enter_context(nc.semaphore("st_sync"))
            with nc.Block("setup") as blk:
                @blk.sync
                def _(b):
                    b.dma_start(out=murs[:, :], in_=murd[:, :]).then_inc(st_sync, 16)
                    b.dma_start(out=muks[:, :], in_=mukd[:, :]).then_inc(st_sync, 16)
                    b.dma_start(out=qmask_sb[0:1, :], in_=qmaskd[:, :]).then_inc(st_sync, 16)
                    b.wait_ge(st_sync, 48)

                @blk.gpsimd
                def _(b):
                    b.memset(onesrow[:, :], 1.0)
                    b.memset(onescol128[:, :], 1.0)
                    b.memset(onescol64[:, :], 1.0)
                    b.memset(ones65[:, :], 1.0)
                    b.memset(m8[:, :], 0.0)

        with ExitStack() as st:
            xt = _sbuf(st, nc, "p1_xt", [128, 2 * NCH * 410])
            ln1 = _sbuf(st, nc, "p1_ln", [128, NCH * 411])
            xr = _sbuf(st, nc, "p1_xr", [128, NCH * 410])
            xk = _sbuf(st, nc, "p1_xk", [128, NCH * 410])
            rti = _sbuf(st, nc, "p1_r", [128, NCH * 410])
            ka = _sbuf(st, nc, "p1_ka", [128, NCH * 410])
            tmp = _sbuf(st, nc, "p1_tmp", [128, 2 * 512])
            rows = _sbuf(st, nc, "p1_rows", [1, 4 * 512])
            wr_sb = _sbuf(st, nc, "p1_wr", [128, NCH * D])
            wk_sb = _sbuf(st, nc, "p1_wk", [128, NCH * D])
            wv_sb = _sbuf(st, nc, "p1_wv", [128, NCH * D])
            bS1 = _psum(st, nc, "p1_bS1", [128, 512])
            bS2 = _psum(st, nc, "p1_bS2", [128, 512])
            bA = _psum(st, nc, "p1_bA", [128, 512])
            bB = _psum(st, nc, "p1_bB", [128, 512])
            bP0 = _psum(st, nc, "p1_bP0", [128, 512])
            bP1 = _psum(st, nc, "p1_bP1", [128, 512])
            sems = _sems(st, nc, "pa")
            with nc.Block("ffn1") as blk:
                ph = Ph()
                _ffn_phase(ph, xin, xres, SEG_TILES, NSEG, SEG,
                           xt, ln1, xr, xk, rti, ka, tmp, rows,
                           wr_sb, wk_sb, wv_sb, wts["wrT"], wts["wkfT"],
                           wts["wvfT"], bS1, bS2, bA, bB, bP0, bP1,
                           onesrow, onescol128, murs, muks, m8)
                ph.realize(nc, blk, sems)

        with ExitStack() as st:
            xt = _sbuf(st, nc, "p2_xt", [128, 2 * NCH * 512])
            ln2 = _sbuf(st, nc, "p2_ln", [128, 2 * NCH * 512])
            kstage = _sbuf(st, nc, "p2_ks", [128, NCH * 512])
            vstage = _sbuf(st, nc, "p2_vt", [128, 2 * D])
            tmp = _sbuf(st, nc, "p2_tmp", [128, 2 * 512])
            rows = _sbuf(st, nc, "p2_rows", [1, 4 * 512])
            wk_sb = _sbuf(st, nc, "p2_wk", [128, NCH * D])
            wv_sb = _sbuf(st, nc, "p2_wv", [128, NCH * D])
            wq_sb = _sbuf(st, nc, "p2_wq", [128, NCH * D])
            bS1 = _psum(st, nc, "p2_bS1", [128, 512])
            bS2 = _psum(st, nc, "p2_bS2", [128, 512])
            bA = _psum(st, nc, "p2_bA", [128, 512])
            bB = _psum(st, nc, "p2_bB", [128, 512])
            bP0 = _psum(st, nc, "p2_bP0", [128, 512])
            bP1 = _psum(st, nc, "p2_bP1", [128, 512])
            sems = _sems(st, nc, "pb")
            with nc.Block("kvq") as blk:
                ph = Ph()
                for c in range(NCH):
                    for wsb, wd in ((wk_sb, wts["wkT"]), (wv_sb, wts["wvT"]),
                                    (wq_sb, wts["wqT"])):
                        ph.emit("sync", lambda b, wsb=wsb, wd=wd, c=c: b.dma_start(
                            out=wsb[:, c * D:(c + 1) * D],
                            in_=wd[c * 128:(c + 1) * 128, :]))
                seq2 = []
                for seg in range(NSEG):
                    for (s_, w) in KEY_TILES:
                        seq2.append(("kv", seg * SEG + s_, w,
                                     seg * 2048 + (s_ - 2)))
                for (s_, w) in Q_TILES:
                    seq2.append(("q", s_, w, s_))
                n2 = len(seq2)
                xt_free = {}

                def xb2(i):
                    return xt[:, (i % 2) * NCH * 512:(i % 2) * NCH * 512 + NCH * 512]

                def lb2(i):
                    return ln2[:, (i % 2) * NCH * 512:(i % 2) * NCH * 512 + NCH * 512]

                rows_mark2 = {}
                ln_mark2 = {}

                def loads2(i):
                    kind, col0, w, aux = seq2[i]
                    xb = xb2(i)
                    wt = dict(xt_free.get(i - 2, {}))
                    q = "sync" if i % 2 == 0 else "syncb"
                    for c in range(NCH):
                        ph.emit(q, lambda b, c=c, col0=col0, w=w, xb=xb:
                                b.dma_start(
                                    out=xb[:, c * w:(c + 1) * w],
                                    in_=xres[c * 128:(c + 1) * 128,
                                             col0:col0 + w]), waits=wt)

                def stats2(i):
                    kind, col0, w, aux = seq2[i]
                    rows_mark2[i] = _emit_ln_stats(
                        ph, w, xb2(i), tmp, rows, bS1, bS2, onescol128,
                        ldq=("sync" if i % 2 == 0 else "syncb"))

                def lnapply(i):
                    kind, col0, w, aux = seq2[i]
                    _emit_ln_apply(ph, w, xb2(i), tmp, rows, bA, bB, onesrow,
                                   lb2(i), 0, w, rows_mark=rows_mark2[i])
                    xt_free[i] = {"vector": ph.snap()["vector"]}
                    ln_mark2[i] = ph.snap()["vector"]

                def projK(i, prev_free):
                    kind, col0, w, aux = seq2[i]
                    lnb = lb2(i)
                    if kind == "q":
                        def q_epi(o, bank, aux=aux, w=w):
                            oc = kstage[:, o * w:(o + 1) * w]
                            ph.emit("scalar", lambda b, oc=oc, bank=bank, w=w:
                                    b.copy(out=oc, in_=bank[:, 0:w]),
                                    waits=["tensor", "gpsimd"])
                            ph.emit("gpsimd", lambda b, oc=oc, o=o, aux=aux, w=w:
                                    b.dma_start(
                                        out=qscr[o * 128:(o + 1) * 128,
                                                 aux:aux + w], in_=oc),
                                    waits=["scalar"])
                        return _emit_proj(
                            ph, wq_sb, lnb, w, (bP0, bP1), q_epi, "scalar",
                            rhs_wait={"vector": ln_mark2[i]},
                            prev_free=prev_free)
                    keybase = aux

                    def k_epi(o, bank, keybase=keybase, w=w):
                        oc = kstage[:, o * w:(o + 1) * w]
                        ph.emit("scalar", lambda b, oc=oc, bank=bank, w=w:
                                b.copy(out=oc, in_=bank[:, 0:w]),
                                waits=["tensor", "gpsimd"])
                        ph.emit("gpsimd", lambda b, oc=oc, o=o, keybase=keybase,
                                w=w: b.dma_start(
                                    out=kscr[o * 128:(o + 1) * 128,
                                             keybase:keybase + w], in_=oc),
                                waits=["scalar"])
                    return _emit_proj(
                        ph, wk_sb, lnb, w, (bP0, bP1), k_epi, "scalar",
                        rhs_wait={"vector": ln_mark2[i]}, prev_free=prev_free)

                def projV(i, prev_free):
                    kind, col0, w, aux = seq2[i]
                    if kind == "q":
                        return prev_free
                    lnb = lb2(i)
                    keybase = aux
                    last_sc = None
                    for tsub in range(w // 128):
                        vbase = (tsub % 2) * D
                        for dvh in range(2):
                            bank = (bP0, bP1)[dvh]
                            wt = {"scalar": ph.snap()["scalar"]}
                            for c in range(NCH):
                                lhs = lnb[:, c * w + tsub * 128:
                                          c * w + tsub * 128 + 128]
                                rhsw = wv_sb[:, c * D + dvh * 512:
                                             c * D + dvh * 512 + 512]
                                ph.emit("tensor", lambda b, lhs=lhs, rhsw=rhsw,
                                        c=c, bank=bank: b.matmul(
                                            bank[:, :], lhs, rhsw,
                                            start=(c == 0),
                                            stop=(c == NCH - 1)),
                                        waits=(wt if c == 0 else None))
                            oc = vstage[:, vbase + dvh * 512:
                                        vbase + dvh * 512 + 512]
                            ph.emit("scalar", lambda b, oc=oc, bank=bank: b.copy(
                                out=oc, in_=bank[:, :]),
                                waits=["tensor", "gpsimd"])
                        krow = keybase + tsub * 128
                        for hd_ in range(NH):
                            srcv = vstage[:, vbase + hd_ * HD:
                                          vbase + hd_ * HD + HD]
                            ph.emit("gpsimd", lambda b, srcv=srcv, hd_=hd_,
                                    krow=krow: b.dma_start(
                                        out=vscr[hd_, krow:krow + 128, :],
                                        in_=srcv), waits=["scalar"])
                    return {"scalar": ph.snap()["scalar"]}

                loads2(0)
                stats2(0)
                lnapply(0)
                pf2 = None
                for i in range(n2):
                    if i + 1 < n2:
                        loads2(i + 1)
                    fk2 = projK(i, pf2)
                    if i + 1 < n2:
                        stats2(i + 1)
                        lnapply(i + 1)
                    pf2 = projV(i, {"scalar": max(fk2[NCH - 2], fk2[NCH - 1])})
                    if isinstance(pf2, dict) and "scalar" not in pf2:
                        pf2 = None
                ph.realize(nc, blk, sems)

        with ExitStack() as st:
            ktile = _sbuf(st, nc, "a_k", [128, 2 * NKEY])
            qtile = _sbuf(st, nc, "a_q", [128, 2 * SEG])
            vtileA = _sbuf(st, nc, "a_va", [128, 2 * 2048])
            vtileB = _sbuf(st, nc, "a_vb", [128, 2 * 2048])
            ksq = _sbuf(st, nc, "a_ksq", [128, NKEY])
            qsq = _sbuf(st, nc, "a_qsq", [128, SEG])
            krowA = _sbuf(st, nc, "a_krowa", [1, NKEY])
            krowB = _sbuf(st, nc, "a_krowb", [1, NKEY])
            mk2A = _sbuf(st, nc, "a_mk2a", [1, 1])
            mk2B = _sbuf(st, nc, "a_mk2b", [1, 1])
            negcA = _sbuf(st, nc, "a_negca", [1, SEG])
            negcB = _sbuf(st, nc, "a_negcb", [1, SEG])
            cbcAB = _sbuf(st, nc, "a_cbc", [128, 2 * 1024])
            ssub = _sbuf(st, nc, "a_ss", [128, 2 * 1024])
            pring = _sbuf(st, nc, "a_pp", [128, 2 * 1024])
            sacc = _sbuf(st, nc, "a_sc", [128, 1024])
            rrowA = _sbuf(st, nc, "a_rra", [1, 512])
            rrowB = _sbuf(st, nc, "a_rrb", [1, 512])
            rbc_sb = _sbuf(st, nc, "a_rbc", [128, 512])
            ostg = _sbuf(st, nc, "a_ostg", [128, 512])
            sAB0 = _psum(st, nc, "a_s01", [128, 1024])
            sAB1 = _psum(st, nc, "a_s23", [128, 1024])
            obank = _psum(st, nc, "a_o", [128, 512])
            bcb = _psum(st, nc, "a_bc", [128, 512])
            csb0 = _psum(st, nc, "a_cs0", [128, 512])
            csb1 = _psum(st, nc, "a_cs1", [128, 512])
            sems = _sems(st, nc, "pc")
            with nc.Block("attn") as blk:
                ph = Ph()
                pair_done = {}
                prev_sums_l = [0]
                for hp in range(NH // 2):
                    hA, hB = 2 * hp, 2 * hp + 1
                    par = hp % 2
                    kof, qof, vof = par * NKEY, par * SEG, par * 2048
                    ldq = "sync" if par == 0 else "syncb"
                    wt = {"tensor": pair_done[hp - 2]} if hp >= 2 else None
                    ph.emit(ldq, lambda b, hp=hp, kof=kof: b.dma_start(
                        out=ktile[:, kof:kof + NKEY],
                        in_=kscr[hp * 128:(hp + 1) * 128, :]), waits=wt)
                    ph.emit(ldq, lambda b, hp=hp, qof=qof: b.dma_start(
                        out=qtile[:, qof:qof + SEG],
                        in_=qscr[hp * 128:(hp + 1) * 128, :]))
                    for vt, hh in ((vtileA, hA), (vtileB, hB)):
                        for jq in range(4):
                            def vload(b, vt=vt, hh=hh, jq=jq, vof=vof):
                                dst = vt[:, vof + jq * 512:vof + (jq + 1) * 512] \
                                    .rearrange("p (j d) -> p j d", d=64)
                                srcv = vscr[hh, jq * 1024:(jq + 1) * 1024, :] \
                                    .rearrange("(j p) d -> p j d", p=128)
                                return b.dma_start(out=dst, in_=srcv)
                            ph.emit(ldq, vload)
                    ld_mark = ph.snap()[ldq]
                    wtv = {ldq: ld_mark}
                    if hp >= 1:
                        wtv["tensor"] = pair_done[hp - 1]
                    ph.emit("vector", lambda b, kof=kof: b.tensor_tensor(
                        ksq[:, :], ktile[:, kof:kof + NKEY],
                        ktile[:, kof:kof + NKEY], AluOpType.mult), waits=wtv)
                    ph.emit("vector", lambda b, qof=qof: b.tensor_tensor(
                        qsq[:, :], qtile[:, qof:qof + SEG],
                        qtile[:, qof:qof + SEG], AluOpType.mult))
                    vm = ph.snap()["vector"]
                    evac_done = {}
                    for j in range(8):
                        wt2 = {"vector": vm if j < 1 else evac_done[j - 1]}
                        if j == 0:
                            wt2["scalar"] = ph.snap()["scalar"]
                        ph.emit("tensor", lambda b, j=j: b.matmul(
                            csb0[0:1, :], onescol128[0:64, 0:1],
                            ksq[0:64, j * 512:(j + 1) * 512],
                            start=True, stop=True), waits=wt2)
                        ph.emit("tensor", lambda b, j=j: b.matmul(
                            csb1[0:1, :], onescol128[64:128, 0:1],
                            ksq[64:128, j * 512:(j + 1) * 512],
                            start=True, stop=True))
                        ph.emit("vector", lambda b, j=j: b.tensor_copy(
                            krowA[0:1, j * 512:(j + 1) * 512], csb0[0:1, :]),
                            waits=["tensor"])
                        ph.emit("vector", lambda b, j=j: b.tensor_copy(
                            krowB[0:1, j * 512:(j + 1) * 512], csb1[0:1, :]))
                        evac_done[j] = ph.snap()["vector"]
                    ph.emit("vector", lambda b: b.reduce_max(
                        mk2A[0:1, 0:1], krowA[0:1, :], axis=AX.X))
                    ph.emit("vector", lambda b: b.reduce_max(
                        mk2B[0:1, 0:1], krowB[0:1, :], axis=AX.X))
                    ph.emit("vector", lambda b: b.tensor_copy(
                        ksq[:, 0:512], ksq[:, 0:512]))
                    vm2 = ph.snap()["vector"]
                    for j, (qs, qw) in enumerate(QB_LIST):
                        wt2 = {"vector": vm2 if j < 1 else evac_done[8 + j - 1]}
                        ph.emit("tensor", lambda b, qs=qs, qw=qw: b.matmul(
                            csb0[0:1, 0:qw], onescol128[0:64, 0:1],
                            qsq[0:64, qs:qs + qw], start=True, stop=True),
                            waits=wt2)
                        ph.emit("tensor", lambda b, qs=qs, qw=qw: b.matmul(
                            csb1[0:1, 0:qw], onescol128[64:128, 0:1],
                            qsq[64:128, qs:qs + qw], start=True, stop=True))
                        ph.emit("vector", lambda b, qs=qs, qw=qw:
                                b.tensor_scalar(
                                    negcA[0:1, qs:qs + qw], csb0[0:1, 0:qw],
                                    mk2A[0:1, 0:1], -1.0 / 16.0,
                                    AluOpType.add, AluOpType.mult),
                                waits=["tensor"])
                        ph.emit("vector", lambda b, qs=qs, qw=qw:
                                b.tensor_scalar(
                                    negcB[0:1, qs:qs + qw], csb1[0:1, 0:qw],
                                    mk2B[0:1, 0:1], -1.0 / 16.0,
                                    AluOpType.add, AluOpType.mult))
                        evac_done[8 + j] = ph.snap()["vector"]
                    prev_subs = None
                    norm_mark = {}
                    prev_norm_g = {"v": ph.snap()["vector"]}
                    for qbi, (qs, qw) in enumerate(QB_LIST):
                        cb = cbcAB[:, (qbi % 2) * 1024:(qbi % 2) * 1024 + 1024]
                        ob = (obank, bcb)[qbi % 2]
                        wtc = {"vector": (prev_subs[1] if prev_subs
                                          else ph.snap()["vector"]),
                               "scalar": ph.snap()["scalar"]}
                        ph.emit("tensor", lambda b, qs=qs, qw=qw: b.matmul(
                            sAB1[:, 0:qw], onesrow[0:1, :],
                            negcA[0:1, qs:qs + qw],
                            start=True, stop=True), waits=wtc)
                        ph.emit("scalar", lambda b, qw=qw, cb=cb: b.copy(
                            out=cb[:, 0:qw], in_=sAB1[:, 0:qw]),
                            waits=["tensor"])
                        ph.emit("tensor", lambda b, qs=qs, qw=qw: b.matmul(
                            sAB1[:, 512:512 + qw], onesrow[0:1, :],
                            negcB[0:1, qs:qs + qw],
                            start=True, stop=True))
                        ph.emit("scalar", lambda b, qw=qw, cb=cb: b.copy(
                            out=cb[:, 512:512 + qw], in_=sAB1[:, 512:512 + qw]),
                            waits=["tensor"])
                        cbc_mark = ph.snap()["scalar"]
                        S_mm, S_sub, S_exp, S_sac = {}, {}, {}, {}
                        S_avA, S_avB = {}, {}
                        prev_store = ph.snap()["gpsimd"]

                        for j in range(32):
                            slot = j % 2
                            bk = (sAB0, sAB1)[slot]
                            if j >= 2:
                                bkfree = S_sub[j - 2]
                            elif prev_subs:
                                bkfree = prev_subs[j]
                            else:
                                bkfree = 0
                            wt2 = {"scalar": cbc_mark}
                            if bkfree:
                                wt2["vector"] = bkfree
                            ph.emit("tensor", lambda b, j=j, bk=bk, kof=kof,
                                    qof=qof, qs=qs, qw=qw: b.matmul(
                                        bk[:, 0:qw],
                                        ktile[0:64, kof + j * 128:kof + (j + 1) * 128],
                                        qtile[0:64, qof + qs:qof + qs + qw],
                                        start=True, stop=True,
                                        tile_position=(0, 0)), waits=wt2)
                            ph.emit("tensor", lambda b, j=j, bk=bk, kof=kof,
                                    qof=qof, qs=qs, qw=qw: b.matmul(
                                        bk[:, 512:512 + qw],
                                        ktile[64:128, kof + j * 128:kof + (j + 1) * 128],
                                        qtile[64:128, qof + qs:qof + qs + qw],
                                        start=True, stop=True,
                                        tile_position=(64, 0)))
                            S_mm[j] = ph.snap()["tensor"]
                            wt3 = {"tensor": S_mm[j]}
                            if j >= 2:
                                wt3["scalar"] = S_exp[j - 2]
                            ph.emit("vector", lambda b, j=j, bk=bk, qw=qw,
                                    cb=cb: b.scalar_tensor_tensor(
                                        abview(ssub, 2 * (j % 2) + 0, qw, 512)
                                        if False else
                                        ssub[:, (j % 2) * 1024:(j % 2 + 1) * 1024]
                                        .rearrange("p (h q) -> p h q", q=512)[:, :, 0:qw],
                                        bk[:, 0:1024]
                                        .rearrange("p (h q) -> p h q", q=512)[:, :, 0:qw],
                                        0.125,
                                        cb[:, 0:1024]
                                        .rearrange("p (h q) -> p h q", q=512)[:, :, 0:qw],
                                        AluOpType.mult, AluOpType.add),
                                    waits=wt3, dr=False)
                            S_sub[j] = ph.snap()["vector"]
                            wt4 = {"vector": S_sub[j]}
                            if j >= 2:
                                wt4["tensor"] = S_avB[j - 2]
                            ph.emit("scalar", lambda b, j=j, qw=qw: b.activation(
                                pring[:, (j % 2) * 1024:(j % 2 + 1) * 1024]
                                .rearrange("p (h q) -> p h q", q=512)[:, :, 0:qw],
                                ssub[:, (j % 2) * 1024:(j % 2 + 1) * 1024]
                                .rearrange("p (h q) -> p h q", q=512)[:, :, 0:qw],
                                AF.Exp), waits=wt4, dr=False)
                            S_exp[j] = ph.snap()["scalar"]
                            if j == 0:
                                ph.emit("vector", lambda b, qw=qw: b.tensor_copy(
                                    sacc[:, 0:1024]
                                    .rearrange("p (h q) -> p h q", q=512)[:, :, 0:qw],
                                    pring[:, 0:1024]
                                    .rearrange("p (h q) -> p h q", q=512)[:, :, 0:qw]),
                                    waits={"scalar": S_exp[0],
                                           "tensor": prev_sums_l[0]}, dr=False)
                            else:
                                ph.emit("vector", lambda b, j=j, qw=qw:
                                        b.tensor_tensor(
                                            sacc[:, 0:1024]
                                            .rearrange("p (h q) -> p h q", q=512)[:, :, 0:qw],
                                            sacc[:, 0:1024]
                                            .rearrange("p (h q) -> p h q", q=512)[:, :, 0:qw],
                                            pring[:, (j % 2) * 1024:(j % 2 + 1) * 1024]
                                            .rearrange("p (h q) -> p h q", q=512)[:, :, 0:qw],
                                            AluOpType.add),
                                        waits={"scalar": S_exp[j],
                                               "vector": S_sac[j - 1]}, dr=False)
                            S_sac[j] = ph.snap()["vector"]
                            if j >= 1:
                                jj = j - 1
                                wt5 = {"scalar": S_exp[jj]}
                                if jj == 0:
                                    wt5["vector"] = prev_norm_g["v"]
                                ph.emit("tensor", lambda b, jj=jj, vof=vof, qw=qw,
                                        ob=ob: b.matmul(
                                            ob[0:64, 0:qw],
                                            vtileA[:, vof + jj * 64:vof + (jj + 1) * 64],
                                            pring[:, (jj % 2) * 1024:(jj % 2) * 1024 + qw],
                                            start=(jj == 0), stop=False,
                                            tile_position=(0, 0),
                                            skip_group_check=True), waits=wt5)
                                S_avA[jj] = ph.snap()["tensor"]
                                ph.emit("tensor", lambda b, jj=jj, vof=vof, qw=qw,
                                        ob=ob: b.matmul(
                                            ob[64:128, 0:qw],
                                            vtileB[:, vof + jj * 64:vof + (jj + 1) * 64],
                                            pring[:, (jj % 2) * 1024 + 512:
                                                  (jj % 2) * 1024 + 512 + qw],
                                            start=(jj == 0), stop=False,
                                            tile_position=(0, 64),
                                            skip_group_check=True))
                                S_avB[jj] = ph.snap()["tensor"]
                        for jj in (31,):
                            ph.emit("tensor", lambda b, jj=jj, vof=vof, qw=qw:
                                    b.matmul(
                                        obank[0:64, 0:qw],
                                        vtileA[:, vof + jj * 64:vof + (jj + 1) * 64],
                                        pring[:, (jj % 2) * 1024:(jj % 2) * 1024 + qw],
                                        start=False, stop=True,
                                        tile_position=(0, 0),
                                        skip_group_check=True),
                                    waits={"scalar": S_exp[jj]})
                            S_avA[jj] = ph.snap()["tensor"]
                            ph.emit("tensor", lambda b, jj=jj, vof=vof, qw=qw:
                                    b.matmul(
                                        obank[64:128, 0:qw],
                                        vtileB[:, vof + jj * 64:vof + (jj + 1) * 64],
                                        pring[:, (jj % 2) * 1024 + 512:
                                              (jj % 2) * 1024 + 512 + qw],
                                        start=False, stop=True,
                                        tile_position=(0, 64),
                                        skip_group_check=True))
                            S_avB[jj] = ph.snap()["tensor"]
                        ph.emit("tensor", lambda b, qw=qw: b.matmul(
                            csb0[0:1, 0:qw], onescol128[:, 0:1], sacc[:, 0:qw],
                            start=True, stop=True),
                            waits={"vector": S_sac[31],
                                   "scalar": ph.snap()["scalar"]})
                        ph.emit("tensor", lambda b, qw=qw: b.matmul(
                            csb1[0:1, 0:qw], onescol128[:, 0:1],
                            sacc[:, 512:512 + qw],
                            start=True, stop=True))
                        prev_sums_l[0] = ph.snap()["tensor"]
                        ph.emit("vector", lambda b, qw=qw: b.reciprocal(
                            rrowA[0:1, 0:qw], csb0[0:1, 0:qw]),
                            waits={"tensor": prev_sums_l[0]})
                        ph.emit("vector", lambda b, qs=qs, qw=qw: b.tensor_tensor(
                            rrowA[0:1, 0:qw], rrowA[0:1, 0:qw],
                            qmask_sb[0:1, qs - 1:qs - 1 + qw], AluOpType.mult))
                        ph.emit("vector", lambda b, qw=qw: b.reciprocal(
                            rrowB[0:1, 0:qw], csb1[0:1, 0:qw]))
                        ph.emit("vector", lambda b, qs=qs, qw=qw: b.tensor_tensor(
                            rrowB[0:1, 0:qw], rrowB[0:1, 0:qw],
                            qmask_sb[0:1, qs - 1:qs - 1 + qw], AluOpType.mult))
                        ph.emit("tensor", lambda b, qw=qw: b.matmul(
                            csb0[0:64, 0:qw], onesrow[0:1, 0:64],
                            rrowA[0:1, 0:qw],
                            start=True, stop=True, tile_position=(0, 0),
                            skip_group_check=True), waits=["vector"])
                        ph.emit("tensor", lambda b, qw=qw: b.matmul(
                            csb1[64:128, 0:qw], onesrow[0:1, 0:64],
                            rrowB[0:1, 0:qw],
                            start=True, stop=True, tile_position=(0, 64),
                            skip_group_check=True))
                        ph.emit("scalar", lambda b, qw=qw: b.copy(
                            out=rbc_sb[0:64, 0:qw], in_=csb0[0:64, 0:qw]),
                            waits=["tensor"])
                        ph.emit("scalar", lambda b, qw=qw: b.copy(
                            out=rbc_sb[64:128, 0:qw], in_=csb1[64:128, 0:qw]))
                        ph.emit("vector", lambda b, qw=qw, ob=ob: b.tensor_tensor(
                            ostg[:, 0:qw], ob[:, 0:qw], rbc_sb[:, 0:qw],
                            AluOpType.mult),
                            waits={"scalar": ph.snap()["scalar"],
                                   "gpsimd": prev_store,
                                   "tensor": S_avB[31]})
                        norm_mark[qbi] = ph.snap()["vector"]
                        ph.emit("gpsimd", lambda b, hp=hp, qs=qs, qw=qw: b.dma_start(
                            out=oscr[hp * 128:(hp + 1) * 128, qs - 1:qs - 1 + qw],
                            in_=ostg[:, 0:qw]), waits=["vector"])
                        prev_subs = (S_sub[30], S_sub[31])
                        prev_norm_g["v"] = norm_mark[qbi]
                    pair_done[hp] = ph.snap()["tensor"]
                ph.realize(nc, blk, sems)

        with ExitStack() as st:
            otile = _sbuf(st, nc, "c_o", [128, NCH * 512])
            xtile = _sbuf(st, nc, "c_x", [128, NCH * 512])
            x2tile = _sbuf(st, nc, "c_x2", [128, NCH * 512])
            wo_sb = _sbuf(st, nc, "c_wo", [128, NCH * D])
            bP0 = _psum(st, nc, "c_bP0", [128, 512])
            bP1 = _psum(st, nc, "c_bP1", [128, 512])
            sems = _sems(st, nc, "pd")
            with nc.Block("wo") as blk:
                ph = Ph()
                for c in range(NCH):
                    ph.emit("sync", lambda b, c=c: b.dma_start(
                        out=wo_sb[:, c * D:(c + 1) * D],
                        in_=wts["woT"][c * 128:(c + 1) * 128, :]))
                for (s, w) in F3_TILES:
                    ld_prev = ph.snap()
                    for c in range(NCH):
                        ph.emit("sync", lambda b, c=c, s=s, w=w: b.dma_start(
                            out=otile[:, c * w:(c + 1) * w],
                            in_=oscr[c * 128:(c + 1) * 128, s:s + w]),
                            waits={"tensor": ld_prev["tensor"]})
                        ph.emit("sync", lambda b, c=c, s=s, w=w: b.dma_start(
                            out=xtile[:, c * w:(c + 1) * w],
                            in_=xres[c * 128:(c + 1) * 128, 1 + s:1 + s + w]),
                            waits={"vector": ld_prev["vector"]})

                    def wo_epi(o, bank, s=s, w=w):
                        ph.emit("vector", lambda b, o=o, bank=bank, w=w:
                                b.tensor_tensor(
                                    x2tile[:, o * w:(o + 1) * w], bank[:, 0:w],
                                    xtile[:, o * w:(o + 1) * w], AluOpType.add),
                                waits=["tensor", "gpsimd"])
                        ph.emit("gpsimd", lambda b, o=o, s=s, w=w: b.dma_start(
                            out=x2scr[o * 128:(o + 1) * 128, s:s + w],
                            in_=x2tile[:, o * w:(o + 1) * w]), waits=["vector"])
                    ph.emit("vector", lambda b: b.memset(x2tile[0:1, 0:1], 0.0),
                            waits=["sync"])
                    _emit_proj(ph, wo_sb, otile, w, (bP0, bP1), wo_epi, "vector")
                ph.realize(nc, blk, sems)

        with ExitStack() as st:
            xt = _sbuf(st, nc, "d_xt", [128, 2 * NCH * 410])
            ln1 = _sbuf(st, nc, "d_ln", [128, NCH * 411])
            xr = _sbuf(st, nc, "d_xr", [128, NCH * 410])
            xk = _sbuf(st, nc, "d_xk", [128, NCH * 410])
            rti = _sbuf(st, nc, "d_r", [128, NCH * 410])
            ka = _sbuf(st, nc, "d_ka", [128, NCH * 410])
            tmp = _sbuf(st, nc, "d_tmp", [128, 2 * 512])
            rows = _sbuf(st, nc, "d_rows", [1, 4 * 512])
            wr_sb = _sbuf(st, nc, "d_wr", [128, NCH * D])
            wk_sb = _sbuf(st, nc, "d_wk", [128, NCH * D])
            wv_sb = _sbuf(st, nc, "d_wv", [128, NCH * D])
            bS1 = _psum(st, nc, "d_bS1", [128, 512])
            bS2 = _psum(st, nc, "d_bS2", [128, 512])
            bA = _psum(st, nc, "d_bA", [128, 512])
            bB = _psum(st, nc, "d_bB", [128, 512])
            bP0 = _psum(st, nc, "d_bP0", [128, 512])
            bP1 = _psum(st, nc, "d_bP1", [128, 512])
            sems = _sems(st, nc, "pe")
            with nc.Block("ffn3") as blk:
                ph = Ph()
                _ffn_phase(ph, x2scr, y, F3_TILES, 1, NQ,
                           xt, ln1, xr, xk, rti, ka, tmp, rows,
                           wr_sb, wk_sb, wv_sb, wts["wrT"], wts["wkfT"],
                           wts["wvfT"], bS1, bS2, bA, bB, bP0, bP1,
                           onesrow, onescol128, murs, muks, m8, out_shift=1)
                ph.realize(nc, blk, sems)

    return nc


def _host_prepare(inputs):
    x = np.asarray(inputs["L0_attn_x"], dtype=np.float32)
    wT = {}
    for nm, key in (("wqT", "L0_attn_Wq"), ("wkT", "L0_attn_Wk"),
                    ("wvT", "L0_attn_Wv"), ("woT", "L0_attn_Wo"),
                    ("wrT", "L0_ffn_W_r"), ("wkfT", "L0_ffn_W_k"),
                    ("wvfT", "L0_ffn_W_v")):
        wT[nm] = np.ascontiguousarray(np.asarray(inputs[key], np.float32).T)
    mur = np.ascontiguousarray(
        np.asarray(inputs["L0_ffn_mu_r"], np.float32).reshape(NCH, 128).T)
    muk = np.ascontiguousarray(
        np.asarray(inputs["L0_ffn_mu_k"], np.float32).reshape(NCH, 128).T)
    in_maps = []
    for core in range(8):
        b, h = core // 2, core % 2
        xf = np.ascontiguousarray(x[b].T)
        half = [xf[:, 0:2048], xf[:, 2048:4096]]
        ctx = [np.zeros((D, 2), np.float32), np.ascontiguousarray(xf[:, 2046:2048])]
        own, oth = h, 1 - h
        segA = np.concatenate([ctx[own], half[own]], axis=1)
        segB = np.concatenate([ctx[oth], half[oth]], axis=1)
        xin = np.ascontiguousarray(np.concatenate([segA, segB], axis=1))
        qmask = np.ones((1, NQ), np.float32)
        if h == 0:
            qmask[0, 0] = 0.0
        im = {"xin": xin, "mur": mur, "muk": muk, "qmask": qmask}
        im.update(wT)
        in_maps.append(im)
    return in_maps


def kernel(**inputs):
    in_maps = _host_prepare(inputs)
    nc = build_nc()
    res = run_bass_kernel_spmd(nc, in_maps, core_ids=list(range(8)))
    out = np.empty((4, 4096, 1024), np.float32)
    for core in range(8):
        b, h = core // 2, core % 2
        out[b, 2048 * h:2048 * (h + 1), :] = res.results[core]["y"].T
    return out


if __name__ == "__main__":
    import time
    t0 = time.time()
    nc = build_nc()
    print(f"built ok in {time.time() - t0:.1f}s")


# version 18
# speedup vs baseline: 1.2071x; 1.0177x over previous
from contextlib import ExitStack

import numpy as np
import concourse.bass as bass
import concourse.mybir as mybir
from concourse.alu_op_type import AluOpType
from concourse.bass_utils import run_bass_kernel_spmd

F32 = mybir.dt.float32
AF = mybir.ActivationFunctionType
AX = mybir.AxisListType

D = 1024
NCH = 8
SEG = 2050
NSEG = 2
NKEY = 4096
NQ = 2049
NH = 16
HD = 64
EPS = 1e-5
_DEBUG = False

SEG_TILES = [(0, 410), (410, 410), (820, 410), (1230, 410), (1640, 410)]
KEY_TILES = [(2, 512), (514, 512), (1026, 512), (1538, 512)]
Q_TILES = [(0, 410), (410, 410), (820, 410), (1230, 410), (1640, 410)]
QB_LIST = [(1, 410), (411, 410), (821, 410), (1231, 410), (1641, 409)]
F3_TILES = [(0, 410), (410, 410), (820, 410), (1230, 410), (1640, 409)]

ENGINES = ("tensor", "vector", "scalar", "sync", "syncb", "gpsimd")
ENGINE_OF = {"syncb": "sync"}


class Ph:

    def __init__(self):
        self.steps = {e: [] for e in ENGINES}
        self.cnt = {e: 0 for e in ENGINES}
        self.seq = 0

    def snap(self):
        return dict(self.cnt)

    def emit(self, eng, fn, waits=None, inc=None, dr=True):
        if inc is None:
            inc = 16 if eng in ("sync", "syncb", "gpsimd") else 1
        if isinstance(waits, (list, tuple)):
            waits = {e: self.cnt[e] for e in waits if e != eng}
        waits = {e: v for e, v in (waits or {}).items() if v > 0}
        self.cnt[eng] += inc
        self.seq += 1
        self.steps[eng].append((fn, waits, self.cnt[eng], self.seq, dr))

    def realize(self, nc, block, sems):
        waited = {e: set() for e in ENGINES}
        for e in ENGINES:
            for _, waits, _, _, _ in self.steps[e]:
                for we, wv in waits.items():
                    waited[we].add(wv)
        for e in ("sync", "syncb", "gpsimd"):
            if self.cnt[e] > 0:
                waited[e].add(self.cnt[e])
        plans = {}
        for e in ENGINES:
            if e in ("sync", "syncb", "gpsimd"):
                plan = {}
                prev = 0
                for i, (_, _, after, _, _) in enumerate(self.steps[e]):
                    plan[i] = after - prev
                    prev = after
                plans[e] = plan
                continue
            if getattr(self, "dense", False):
                plan = {}
                prev = 0
                for i, (_, _, after, _, _) in enumerate(self.steps[e]):
                    plan[i] = after - prev
                    prev = after
                plans[e] = plan
                continue
            targets = sorted(waited[e])
            plan = {}
            prev = 0
            ti = 0
            for i, (_, _, after, _, _) in enumerate(self.steps[e]):
                amt = 0
                while ti < len(targets) and targets[ti] <= after:
                    amt += targets[ti] - prev
                    prev = targets[ti]
                    ti += 1
                if amt:
                    plan[i] = amt
            plans[e] = plan

        def body(phys):
            merged = []
            for e in ENGINES:
                if ENGINE_OF.get(e, e) == phys:
                    for i, step in enumerate(self.steps[e]):
                        merged.append((step[3], e, i, step))
            merged.sort()

            def run(builder):
                for _, e, i, (fn, waits, _, _, dr) in merged:
                    for we, wv in waits.items():
                        builder.wait_ge(sems[we], wv)
                    inst = fn(builder)
                    if i in plans[e]:
                        assert inst is not None, f"{e} step {i} returned None"
                        inst.then_inc(sems[e], plans[e][i])
                    if dr and phys in ("vector", "scalar"):
                        builder.drain()
                for e in ENGINES:
                    if ENGINE_OF.get(e, e) == phys and \
                            e in ("sync", "syncb", "gpsimd") and self.cnt[e] > 0:
                        builder.wait_ge(sems[e], self.cnt[e])
            return run

        for phys in ("tensor", "vector", "scalar", "sync", "gpsimd"):
            if any(self.steps[e] and ENGINE_OF.get(e, e) == phys
                   for e in ENGINES):
                getattr(block, phys)(body(phys))


def _emit_ln_stats(ph, w, xt, tmp, rows, bankS1, bankS2, onescol128,
                   first_vec_waits=None, ldq="sync"):
    mu = rows[0:1, 0:w]
    t = rows[0:1, 512:512 + w]
    var = rows[0:1, 1024:1024 + w]
    rstd = rows[0:1, 1536:1536 + w]
    for c in range(NCH):
        xc = xt[:, c * w:(c + 1) * w]
        ph.emit("tensor", lambda b, xc=xc, c=c: b.matmul(
            bankS1[0:1, 0:w], onescol128[:, 0:1], xc,
            start=(c == 0), stop=(c == NCH - 1)),
            waits=(["sync", ldq, "vector"] if c == 0 else None))
    s2_done = {}
    for c in range(NCH):
        xc = xt[:, c * w:(c + 1) * w]
        tb = tmp[:, (c % 2) * 512:(c % 2) * 512 + w]
        if c == 0:
            wt = dict(first_vec_waits or {})
            wt["sync"] = ph.snap()["sync"]
            wt[ldq] = ph.snap()[ldq]
        elif c == 1:
            wt = None
        else:
            wt = {"tensor": s2_done[c - 2]}
        ph.emit("vector", lambda b, xc=xc, tb=tb: b.tensor_tensor(
            tb, xc, xc, AluOpType.mult), waits=wt, dr=True)
        ph.emit("tensor", lambda b, tb=tb, c=c: b.matmul(
            bankS2[0:1, 0:w], onescol128[:, 0:1], tb,
            start=(c == 0), stop=(c == NCH - 1)), waits=["vector"])
        s2_done[c] = ph.snap()["tensor"]
    ph.emit("vector", lambda b: b.tensor_scalar(
        mu, bankS1[0:1, 0:w], 1.0 / D, None, AluOpType.mult), waits=["tensor"])
    ph.emit("vector", lambda b: b.tensor_scalar(
        t, bankS2[0:1, 0:w], 1.0 / D, None, AluOpType.mult))
    ph.emit("vector", lambda b: b.tensor_tensor(var, mu, mu, AluOpType.mult))
    ph.emit("vector", lambda b: b.tensor_tensor(var, t, var, AluOpType.subtract))
    ph.emit("vector", lambda b: b.tensor_scalar(
        var, var, float(EPS), None, AluOpType.add))
    ph.emit("scalar", lambda b: b.activation(t, var, AF.Ln), waits=["vector"])
    ph.emit("vector", lambda b: b.tensor_scalar(
        var, t, -0.5, None, AluOpType.mult), waits=["scalar"])
    ph.emit("scalar", lambda b: b.activation(rstd, var, AF.Exp), waits=["vector"])
    ph.emit("vector", lambda b: b.scalar_tensor_tensor(
        t, mu, -1.0, rstd, AluOpType.mult, AluOpType.mult), waits=["scalar"])
    return ph.snap()["vector"]


def _emit_ln_apply(ph, w, xt, tmp, rows, bankA, bankB, onesrow, ln_out,
                   ln_off, ln_stride, rows_mark=None):
    rstd = rows[0:1, 1536:1536 + w]
    brow = rows[0:1, 512:512 + w]
    wt = {"vector": rows_mark} if rows_mark else ["vector"]
    ph.emit("tensor", lambda b: b.matmul(bankA[:, 0:w], onesrow[0:1, :], rstd,
                                         start=True, stop=True),
            waits=wt)
    ph.emit("tensor", lambda b: b.matmul(bankB[:, 0:w], onesrow[0:1, :], brow,
                                         start=True, stop=True))
    for c in range(NCH):
        xc = xt[:, c * w:(c + 1) * w]
        oc = ln_out[:, c * ln_stride + ln_off: c * ln_stride + ln_off + w]
        tb = tmp[:, (c % 2) * 512:(c % 2) * 512 + w]
        ph.emit("vector", lambda b, xc=xc, tb=tb: b.tensor_tensor(
            tb, xc, bankA[:, 0:w], AluOpType.mult),
            waits=(["tensor"] if c == 0 else None), dr=True)
        ph.emit("vector", lambda b, tb=tb, oc=oc: b.tensor_tensor(
            oc, tb, bankB[:, 0:w], AluOpType.add), dr=True)


def _emit_proj(ph, wsb, rhs_sb, w, banks, epilogue, bank_free_engine,
               rhs_wait=None, prev_free=None):
    if prev_free is None:
        prev_free = {"vector": ph.snap()["vector"],
                     "scalar": ph.snap()["scalar"]}
    free_at = {}
    for o in range(NCH):
        bank = banks[o % 2]
        for c in range(NCH):
            lhs = wsb[:, c * D + o * 128: c * D + o * 128 + 128]
            rc = rhs_sb[:, c * w:(c + 1) * w]
            wt = None
            if c == 0:
                if o < 2:
                    wt = dict(prev_free)
                    if o == 0 and rhs_wait:
                        for k, v in rhs_wait.items():
                            wt[k] = max(wt.get(k, 0), v)
                else:
                    wt = {bank_free_engine: free_at[o - 2]}
            ph.emit("tensor", lambda b, lhs=lhs, rc=rc, c=c, bank=bank, w=w: b.matmul(
                bank[:, 0:w], lhs, rc, start=(c == 0), stop=(c == NCH - 1)),
                waits=wt)
        epilogue(o, bank)
        free_at[o] = ph.snap()[bank_free_engine]
    return free_at


def _ffn_phase(ph, src, dst, tiles, nseg, seg_stride,
               xt, ln1, xr, xk, rti, ka, tmp, rows,
               wr_sb, wk_sb, wv_sb, wrd, wkd, wvd,
               bS1, bS2, bA, bB, bP0, bP1,
               onesrow, onescol128, murs, muks, m8, out_shift=0):
    for c in range(NCH):
        for wsb, wd in ((wr_sb, wrd), (wk_sb, wkd), (wv_sb, wvd)):
            ph.emit("sync", lambda b, wsb=wsb, wd=wd, c=c: b.dma_start(
                out=wsb[:, c * D:(c + 1) * D], in_=wd[c * 128:(c + 1) * 128, :]))
    W = tiles[0][1]
    seq = [(seg, s, w) for seg in range(nseg) for (s, w) in tiles]
    n = len(seq)
    xt_free = {}
    rows_mark = {}
    mix_mark = {}
    ksq_mark = {}

    def xbuf(i):
        return xt[:, (i % 2) * NCH * W:(i % 2) * NCH * W + NCH * W]

    def loads(i):
        seg, st_, w = seq[i]
        col0 = seg * seg_stride + st_
        xb = xbuf(i)
        wt = dict(xt_free.get(i - 2, {}))
        q = "sync" if i % 2 == 0 else "syncb"
        for c in range(NCH):
            ph.emit(q, lambda b, c=c, col0=col0, w=w, xb=xb: b.dma_start(
                out=xb[:, c * w:(c + 1) * w],
                in_=src[c * 128:(c + 1) * 128, col0:col0 + w]), waits=wt)

    def stats(i):
        seg, st_, w = seq[i]
        rows_mark[i] = _emit_ln_stats(ph, w, xbuf(i), tmp, rows, bS1, bS2,
                                      onescol128,
                                      ldq=("sync" if i % 2 == 0 else "syncb"))

    def ln2(i):
        seg, st_, w = seq[i]
        xb = xbuf(i)
        if st_ == tiles[0][0]:
            ph.emit("vector", lambda b: b.memset(m8[:, :], 0.0), inc=1, dr=True)
        _emit_ln_apply(ph, w, xb, tmp, rows, bA, bB, onesrow, ln1, 1, W + 1,
                       rows_mark=rows_mark[i])
        for c in range(NCH):
            ph.emit("vector", lambda b, c=c: b.tensor_copy(
                ln1[:, c * (W + 1):c * (W + 1) + 1], m8[:, c:c + 1]), dr=True)
        for c in range(NCH):
            ph.emit("vector", lambda b, c=c, w=w: b.tensor_copy(
                m8[:, c:c + 1], ln1[:, c * (W + 1) + w:c * (W + 1) + w + 1]),
                dr=True)
        mixwt = {"tensor": ph.snap()["tensor"]}
        for c in range(NCH):
            cur = ln1[:, c * (W + 1) + 1:c * (W + 1) + 1 + w]
            prv = ln1[:, c * (W + 1):c * (W + 1) + w]
            dbuf = tmp[:, (c % 2) * 512:(c % 2) * 512 + w]
            ph.emit("vector", lambda b, cur=cur, prv=prv, dbuf=dbuf:
                    b.tensor_tensor(dbuf, cur, prv, AluOpType.subtract),
                    waits=(mixwt if c == 0 else None), dr=True)
            ph.emit("vector", lambda b, dbuf=dbuf, prv=prv, c=c, w=w:
                    b.scalar_tensor_tensor(xr[:, c * w:(c + 1) * w], dbuf,
                                           murs[:, c:c + 1], prv,
                                           AluOpType.mult, AluOpType.add),
                    dr=True)
            ph.emit("vector", lambda b, dbuf=dbuf, prv=prv, c=c, w=w:
                    b.scalar_tensor_tensor(xk[:, c * w:(c + 1) * w], dbuf,
                                           muks[:, c:c + 1], prv,
                                           AluOpType.mult, AluOpType.add),
                    dr=False)
        mix_mark[i] = ph.snap()["vector"]

    def proj_r(i, prev_free):
        seg, st_, w = seq[i]

        def r_epi(o, bank, w=w):
            ph.emit("scalar", lambda b, o=o, bank=bank, w=w: b.activation(
                rti[:, o * w:(o + 1) * w], bank[:, 0:w], AF.Sigmoid),
                waits=["tensor"])
        return _emit_proj(ph, wr_sb, xr, w, (bP0, bP1), r_epi, "scalar",
                          rhs_wait={"vector": mix_mark[i]},
                          prev_free=prev_free)

    def proj_k(i, prev_free):
        seg, st_, w = seq[i]

        def k_epi(o, bank, w=w):
            ph.emit("scalar", lambda b, o=o, bank=bank, w=w: b.activation(
                ka[:, o * w:(o + 1) * w], bank[:, 0:w], AF.Relu),
                waits=["tensor"])
            ph.emit("vector", lambda b, o=o, w=w: b.tensor_tensor(
                ka[:, o * w:(o + 1) * w], ka[:, o * w:(o + 1) * w],
                ka[:, o * w:(o + 1) * w], AluOpType.mult),
                waits=["scalar"], dr=False)
        fr = _emit_proj(ph, wk_sb, xk, w, (bP0, bP1), k_epi, "scalar",
                        rhs_wait={"vector": mix_mark[i]}, prev_free=prev_free)
        ksq_mark[i] = ph.snap()["vector"]
        return fr

    def proj_v(i, prev_free):
        seg, st_, w = seq[i]
        col0 = seg * seg_stride + st_
        xb = xbuf(i)

        def v_epi(o, bank, w=w, col0=col0, xb=xb):
            ph.emit("vector", lambda b, bank=bank, o=o, w=w:
                    b.tensor_tensor(rti[:, o * w:(o + 1) * w],
                                    rti[:, o * w:(o + 1) * w], bank[:, 0:w],
                                    AluOpType.mult),
                    waits=["tensor", "gpsimd"], dr=True)
            ph.emit("vector", lambda b, o=o, w=w, xb=xb:
                    b.scalar_tensor_tensor(xb[:, o * w:(o + 1) * w],
                                           rti[:, o * w:(o + 1) * w], 0.5,
                                           xb[:, o * w:(o + 1) * w],
                                           AluOpType.mult, AluOpType.add),
                    dr=False)
            dst_c0 = col0 - out_shift
            skip = max(0, -dst_c0)
            if w - skip > 0:
                ph.emit("gpsimd", lambda b, o=o, dst_c0=dst_c0, skip=skip,
                        w=w, xb=xb: b.dma_start(
                            out=dst[o * 128:(o + 1) * 128,
                                    dst_c0 + skip:dst_c0 + w],
                            in_=xb[:, o * w + skip:o * w + w]), waits=["vector"])
        fr = _emit_proj(ph, wv_sb, ka, w, (bP0, bP1), v_epi, "vector",
                        rhs_wait={"vector": ksq_mark[i]}, prev_free=prev_free)
        xt_free[i] = {"vector": ph.snap()["vector"],
                      "gpsimd": ph.snap()["gpsimd"]}
        return fr

    loads(0)
    stats(0)
    ln2(0)
    pf = None
    for i in range(n):
        if i + 1 < n:
            loads(i + 1)
        fr = proj_r(i, pf and {"vector": pf})
        if i + 1 < n:
            stats(i + 1)
        fk = proj_k(i, {"scalar": max(fr[NCH - 2], fr[NCH - 1])})
        if i + 1 < n:
            ln2(i + 1)
        fv = proj_v(i, {"scalar": max(fk[NCH - 2], fk[NCH - 1])})
        pf = max(fv[NCH - 2], fv[NCH - 1])
def _sbuf(st, nc, name, shape):
    return st.enter_context(nc.sbuf_tensor(name, shape, F32))


def _psum(st, nc, name, shape):
    return st.enter_context(nc.psum_tensor(name, shape, F32))


def _sems(st, nc, pfx):
    return {e: st.enter_context(nc.semaphore(f"{pfx}_{e}")) for e in ENGINES}


def build_nc():
    nc = bass.Bass(trn_type="TRN2", target_bir_lowering=False)
    scr_kind = "ExternalOutput" if _DEBUG else "Internal"

    xin = nc.dram_tensor("xin", [D, NSEG * SEG], F32, kind="ExternalInput").ap()
    wts = {}
    for nm in ("wqT", "wkT", "wvT", "woT", "wrT", "wkfT", "wvfT"):
        wts[nm] = nc.dram_tensor(nm, [D, D], F32, kind="ExternalInput").ap()
    murd = nc.dram_tensor("mur", [128, NCH], F32, kind="ExternalInput").ap()
    mukd = nc.dram_tensor("muk", [128, NCH], F32, kind="ExternalInput").ap()
    qmaskd = nc.dram_tensor("qmask", [1, NQ], F32, kind="ExternalInput").ap()
    y = nc.dram_tensor("y", [D, 2048], F32, kind="ExternalOutput").ap()

    xres = nc.dram_tensor("xres", [D, NSEG * SEG], F32, kind=scr_kind).ap()
    kscr = nc.dram_tensor("kscr", [D, NKEY], F32, kind=scr_kind).ap()
    vscr = nc.dram_tensor("vscr", [NH, NKEY, HD], F32, kind=scr_kind).ap()
    qscr = nc.dram_tensor("qscr", [D, SEG], F32, kind=scr_kind).ap()
    oscr = nc.dram_tensor("oscr", [D, NQ], F32, kind=scr_kind).ap()
    x2scr = nc.dram_tensor("x2scr", [D, NQ], F32, kind=scr_kind).ap()

    with ExitStack() as st0:
        murs = _sbuf(st0, nc, "murs", [128, NCH])
        muks = _sbuf(st0, nc, "muks", [128, NCH])
        qmask_sb = _sbuf(st0, nc, "qmask_sb", [1, NQ])
        onesrow = _sbuf(st0, nc, "onesrow", [1, 128])
        onescol128 = _sbuf(st0, nc, "onescol128", [128, 1])
        onescol64 = _sbuf(st0, nc, "onescol64", [64, 1])
        ones65 = _sbuf(st0, nc, "ones65", [65, 64])
        m8 = _sbuf(st0, nc, "m8", [128, NCH])

        with ExitStack() as st:
            st_sync = st.enter_context(nc.semaphore("st_sync"))
            with nc.Block("setup") as blk:
                @blk.sync
                def _(b):
                    b.dma_start(out=murs[:, :], in_=murd[:, :]).then_inc(st_sync, 16)
                    b.dma_start(out=muks[:, :], in_=mukd[:, :]).then_inc(st_sync, 16)
                    b.dma_start(out=qmask_sb[0:1, :], in_=qmaskd[:, :]).then_inc(st_sync, 16)
                    b.wait_ge(st_sync, 48)

                @blk.gpsimd
                def _(b):
                    b.memset(onesrow[:, :], 1.0)
                    b.memset(onescol128[:, :], 1.0)
                    b.memset(onescol64[:, :], 1.0)
                    b.memset(ones65[:, :], 1.0)
                    b.memset(m8[:, :], 0.0)

        with ExitStack() as st:
            xt = _sbuf(st, nc, "p1_xt", [128, 2 * NCH * 410])
            ln1 = _sbuf(st, nc, "p1_ln", [128, NCH * 411])
            xr = _sbuf(st, nc, "p1_xr", [128, NCH * 410])
            xk = _sbuf(st, nc, "p1_xk", [128, NCH * 410])
            rti = _sbuf(st, nc, "p1_r", [128, NCH * 410])
            ka = _sbuf(st, nc, "p1_ka", [128, NCH * 410])
            tmp = _sbuf(st, nc, "p1_tmp", [128, 2 * 512])
            rows = _sbuf(st, nc, "p1_rows", [1, 4 * 512])
            wr_sb = _sbuf(st, nc, "p1_wr", [128, NCH * D])
            wk_sb = _sbuf(st, nc, "p1_wk", [128, NCH * D])
            wv_sb = _sbuf(st, nc, "p1_wv", [128, NCH * D])
            bS1 = _psum(st, nc, "p1_bS1", [128, 512])
            bS2 = _psum(st, nc, "p1_bS2", [128, 512])
            bA = _psum(st, nc, "p1_bA", [128, 512])
            bB = _psum(st, nc, "p1_bB", [128, 512])
            bP0 = _psum(st, nc, "p1_bP0", [128, 512])
            bP1 = _psum(st, nc, "p1_bP1", [128, 512])
            sems = _sems(st, nc, "pa")
            with nc.Block("ffn1") as blk:
                ph = Ph()
                _ffn_phase(ph, xin, xres, SEG_TILES, NSEG, SEG,
                           xt, ln1, xr, xk, rti, ka, tmp, rows,
                           wr_sb, wk_sb, wv_sb, wts["wrT"], wts["wkfT"],
                           wts["wvfT"], bS1, bS2, bA, bB, bP0, bP1,
                           onesrow, onescol128, murs, muks, m8)
                ph.realize(nc, blk, sems)

        with ExitStack() as st:
            xt = _sbuf(st, nc, "p2_xt", [128, 2 * NCH * 512])
            ln2 = _sbuf(st, nc, "p2_ln", [128, 2 * NCH * 512])
            kstage = _sbuf(st, nc, "p2_ks", [128, NCH * 512])
            vstage = _sbuf(st, nc, "p2_vt", [128, 2 * D])
            tmp = _sbuf(st, nc, "p2_tmp", [128, 2 * 512])
            rows = _sbuf(st, nc, "p2_rows", [1, 4 * 512])
            wk_sb = _sbuf(st, nc, "p2_wk", [128, NCH * D])
            wv_sb = _sbuf(st, nc, "p2_wv", [128, NCH * D])
            wq_sb = _sbuf(st, nc, "p2_wq", [128, NCH * D])
            bS1 = _psum(st, nc, "p2_bS1", [128, 512])
            bS2 = _psum(st, nc, "p2_bS2", [128, 512])
            bA = _psum(st, nc, "p2_bA", [128, 512])
            bB = _psum(st, nc, "p2_bB", [128, 512])
            bP0 = _psum(st, nc, "p2_bP0", [128, 512])
            bP1 = _psum(st, nc, "p2_bP1", [128, 512])
            sems = _sems(st, nc, "pb")
            with nc.Block("kvq") as blk:
                ph = Ph()
                for c in range(NCH):
                    for wsb, wd in ((wk_sb, wts["wkT"]), (wv_sb, wts["wvT"]),
                                    (wq_sb, wts["wqT"])):
                        ph.emit("sync", lambda b, wsb=wsb, wd=wd, c=c: b.dma_start(
                            out=wsb[:, c * D:(c + 1) * D],
                            in_=wd[c * 128:(c + 1) * 128, :]))
                seq2 = []
                for seg in range(NSEG):
                    for (s_, w) in KEY_TILES:
                        seq2.append(("kv", seg * SEG + s_, w,
                                     seg * 2048 + (s_ - 2)))
                for (s_, w) in Q_TILES:
                    seq2.append(("q", s_, w, s_))
                n2 = len(seq2)
                xt_free = {}

                def xb2(i):
                    return xt[:, (i % 2) * NCH * 512:(i % 2) * NCH * 512 + NCH * 512]

                def lb2(i):
                    return ln2[:, (i % 2) * NCH * 512:(i % 2) * NCH * 512 + NCH * 512]

                rows_mark2 = {}
                ln_mark2 = {}

                def loads2(i):
                    kind, col0, w, aux = seq2[i]
                    xb = xb2(i)
                    wt = dict(xt_free.get(i - 2, {}))
                    q = "sync" if i % 2 == 0 else "syncb"
                    for c in range(NCH):
                        ph.emit(q, lambda b, c=c, col0=col0, w=w, xb=xb:
                                b.dma_start(
                                    out=xb[:, c * w:(c + 1) * w],
                                    in_=xres[c * 128:(c + 1) * 128,
                                             col0:col0 + w]), waits=wt)

                def stats2(i):
                    kind, col0, w, aux = seq2[i]
                    rows_mark2[i] = _emit_ln_stats(
                        ph, w, xb2(i), tmp, rows, bS1, bS2, onescol128,
                        ldq=("sync" if i % 2 == 0 else "syncb"))

                def lnapply(i):
                    kind, col0, w, aux = seq2[i]
                    _emit_ln_apply(ph, w, xb2(i), tmp, rows, bA, bB, onesrow,
                                   lb2(i), 0, w, rows_mark=rows_mark2[i])
                    xt_free[i] = {"vector": ph.snap()["vector"]}
                    ln_mark2[i] = ph.snap()["vector"]

                def projK(i, prev_free):
                    kind, col0, w, aux = seq2[i]
                    lnb = lb2(i)
                    if kind == "q":
                        def q_epi(o, bank, aux=aux, w=w):
                            oc = kstage[:, o * w:(o + 1) * w]
                            ph.emit("scalar", lambda b, oc=oc, bank=bank, w=w:
                                    b.copy(out=oc, in_=bank[:, 0:w]),
                                    waits=["tensor", "gpsimd"])
                            ph.emit("gpsimd", lambda b, oc=oc, o=o, aux=aux, w=w:
                                    b.dma_start(
                                        out=qscr[o * 128:(o + 1) * 128,
                                                 aux:aux + w], in_=oc),
                                    waits=["scalar"])
                        return _emit_proj(
                            ph, wq_sb, lnb, w, (bP0, bP1), q_epi, "scalar",
                            rhs_wait={"vector": ln_mark2[i]},
                            prev_free=prev_free)
                    keybase = aux

                    def k_epi(o, bank, keybase=keybase, w=w):
                        oc = kstage[:, o * w:(o + 1) * w]
                        ph.emit("scalar", lambda b, oc=oc, bank=bank, w=w:
                                b.copy(out=oc, in_=bank[:, 0:w]),
                                waits=["tensor", "gpsimd"])
                        ph.emit("gpsimd", lambda b, oc=oc, o=o, keybase=keybase,
                                w=w: b.dma_start(
                                    out=kscr[o * 128:(o + 1) * 128,
                                             keybase:keybase + w], in_=oc),
                                waits=["scalar"])
                    return _emit_proj(
                        ph, wk_sb, lnb, w, (bP0, bP1), k_epi, "scalar",
                        rhs_wait={"vector": ln_mark2[i]}, prev_free=prev_free)

                def projV(i, prev_free):
                    kind, col0, w, aux = seq2[i]
                    if kind == "q":
                        return prev_free
                    lnb = lb2(i)
                    keybase = aux
                    last_sc = None
                    for tsub in range(w // 128):
                        vbase = (tsub % 2) * D
                        for dvh in range(2):
                            bank = (bP0, bP1)[dvh]
                            wt = {"scalar": ph.snap()["scalar"]}
                            for c in range(NCH):
                                lhs = lnb[:, c * w + tsub * 128:
                                          c * w + tsub * 128 + 128]
                                rhsw = wv_sb[:, c * D + dvh * 512:
                                             c * D + dvh * 512 + 512]
                                ph.emit("tensor", lambda b, lhs=lhs, rhsw=rhsw,
                                        c=c, bank=bank: b.matmul(
                                            bank[:, :], lhs, rhsw,
                                            start=(c == 0),
                                            stop=(c == NCH - 1)),
                                        waits=(wt if c == 0 else None))
                            oc = vstage[:, vbase + dvh * 512:
                                        vbase + dvh * 512 + 512]
                            ph.emit("scalar", lambda b, oc=oc, bank=bank: b.copy(
                                out=oc, in_=bank[:, :]),
                                waits=["tensor", "gpsimd"])
                        krow = keybase + tsub * 128
                        for hd_ in range(NH):
                            srcv = vstage[:, vbase + hd_ * HD:
                                          vbase + hd_ * HD + HD]
                            ph.emit("gpsimd", lambda b, srcv=srcv, hd_=hd_,
                                    krow=krow: b.dma_start(
                                        out=vscr[hd_, krow:krow + 128, :],
                                        in_=srcv), waits=["scalar"])
                    return {"scalar": ph.snap()["scalar"]}

                loads2(0)
                stats2(0)
                lnapply(0)
                pf2 = None
                for i in range(n2):
                    if i + 1 < n2:
                        loads2(i + 1)
                    fk2 = projK(i, pf2)
                    if i + 1 < n2:
                        stats2(i + 1)
                        lnapply(i + 1)
                    pf2 = projV(i, {"scalar": max(fk2[NCH - 2], fk2[NCH - 1])})
                    if isinstance(pf2, dict) and "scalar" not in pf2:
                        pf2 = None
                ph.realize(nc, blk, sems)

        with ExitStack() as st:
            ktile = _sbuf(st, nc, "a_k", [128, 2 * NKEY])
            qtile = _sbuf(st, nc, "a_q", [128, 2 * SEG])
            vtileA = _sbuf(st, nc, "a_va", [128, 2 * 2048])
            vtileB = _sbuf(st, nc, "a_vb", [128, 2 * 2048])
            ksq = _sbuf(st, nc, "a_ksq", [128, NKEY])
            qsq = _sbuf(st, nc, "a_qsq", [128, SEG])
            krowA = _sbuf(st, nc, "a_krowa", [1, NKEY])
            krowB = _sbuf(st, nc, "a_krowb", [1, NKEY])
            mk2A = _sbuf(st, nc, "a_mk2a", [1, 1])
            mk2B = _sbuf(st, nc, "a_mk2b", [1, 1])
            negcA = _sbuf(st, nc, "a_negca", [1, SEG])
            negcB = _sbuf(st, nc, "a_negcb", [1, SEG])
            cbcAB = _sbuf(st, nc, "a_cbc", [128, 2 * 1024])
            ssub = _sbuf(st, nc, "a_ss", [128, 2 * 1024])
            pring = _sbuf(st, nc, "a_pp", [128, 2 * 1024])
            sacc = _sbuf(st, nc, "a_sc", [128, 1024])
            rrowA = _sbuf(st, nc, "a_rra", [1, 512])
            rrowB = _sbuf(st, nc, "a_rrb", [1, 512])
            rbc_sb = _sbuf(st, nc, "a_rbc", [128, 512])
            ostg = _sbuf(st, nc, "a_ostg", [128, 512])
            sAB0 = _psum(st, nc, "a_s01", [128, 1024])
            sAB1 = _psum(st, nc, "a_s23", [128, 1024])
            obank = _psum(st, nc, "a_o", [128, 512])
            bcb = _psum(st, nc, "a_bc", [128, 512])
            csb0 = _psum(st, nc, "a_cs0", [128, 512])
            csb1 = _psum(st, nc, "a_cs1", [128, 512])
            sems = _sems(st, nc, "pc")
            with nc.Block("attn") as blk:
                ph = Ph()
                pair_done = {}
                prev_sums_l = [0]
                for hp in range(NH // 2):
                    hA, hB = 2 * hp, 2 * hp + 1
                    par = hp % 2
                    kof, qof, vof = par * NKEY, par * SEG, par * 2048
                    ldq = "sync" if par == 0 else "syncb"
                    wt = {"tensor": pair_done[hp - 2]} if hp >= 2 else None
                    ph.emit(ldq, lambda b, hp=hp, kof=kof: b.dma_start(
                        out=ktile[:, kof:kof + NKEY],
                        in_=kscr[hp * 128:(hp + 1) * 128, :]), waits=wt)
                    ph.emit(ldq, lambda b, hp=hp, qof=qof: b.dma_start(
                        out=qtile[:, qof:qof + SEG],
                        in_=qscr[hp * 128:(hp + 1) * 128, :]))
                    for vt, hh in ((vtileA, hA), (vtileB, hB)):
                        for jq in range(4):
                            def vload(b, vt=vt, hh=hh, jq=jq, vof=vof):
                                dst = vt[:, vof + jq * 512:vof + (jq + 1) * 512] \
                                    .rearrange("p (j d) -> p j d", d=64)
                                srcv = vscr[hh, jq * 1024:(jq + 1) * 1024, :] \
                                    .rearrange("(j p) d -> p j d", p=128)
                                return b.dma_start(out=dst, in_=srcv)
                            ph.emit(ldq, vload)
                    ld_mark = ph.snap()[ldq]
                    wtv = {ldq: ld_mark}
                    if hp >= 1:
                        wtv["tensor"] = pair_done[hp - 1]
                    ph.emit("vector", lambda b, kof=kof: b.tensor_tensor(
                        ksq[:, :], ktile[:, kof:kof + NKEY],
                        ktile[:, kof:kof + NKEY], AluOpType.mult), waits=wtv)
                    ph.emit("vector", lambda b, qof=qof: b.tensor_tensor(
                        qsq[:, :], qtile[:, qof:qof + SEG],
                        qtile[:, qof:qof + SEG], AluOpType.mult))
                    vm = ph.snap()["vector"]
                    evac_done = {}
                    for j in range(8):
                        wt2 = {"vector": vm if j < 1 else evac_done[j - 1]}
                        if j == 0:
                            wt2["scalar"] = ph.snap()["scalar"]
                        ph.emit("tensor", lambda b, j=j: b.matmul(
                            csb0[0:1, :], onescol128[0:64, 0:1],
                            ksq[0:64, j * 512:(j + 1) * 512],
                            start=True, stop=True), waits=wt2)
                        ph.emit("tensor", lambda b, j=j: b.matmul(
                            csb1[0:1, :], onescol128[64:128, 0:1],
                            ksq[64:128, j * 512:(j + 1) * 512],
                            start=True, stop=True))
                        ph.emit("vector", lambda b, j=j: b.tensor_copy(
                            krowA[0:1, j * 512:(j + 1) * 512], csb0[0:1, :]),
                            waits=["tensor"])
                        ph.emit("vector", lambda b, j=j: b.tensor_copy(
                            krowB[0:1, j * 512:(j + 1) * 512], csb1[0:1, :]))
                        evac_done[j] = ph.snap()["vector"]
                    ph.emit("vector", lambda b: b.reduce_max(
                        mk2A[0:1, 0:1], krowA[0:1, :], axis=AX.X))
                    ph.emit("vector", lambda b: b.reduce_max(
                        mk2B[0:1, 0:1], krowB[0:1, :], axis=AX.X))
                    ph.emit("vector", lambda b: b.tensor_copy(
                        ksq[:, 0:512], ksq[:, 0:512]))
                    vm2 = ph.snap()["vector"]
                    for j, (qs, qw) in enumerate(QB_LIST):
                        wt2 = {"vector": vm2 if j < 1 else evac_done[8 + j - 1]}
                        ph.emit("tensor", lambda b, qs=qs, qw=qw: b.matmul(
                            csb0[0:1, 0:qw], onescol128[0:64, 0:1],
                            qsq[0:64, qs:qs + qw], start=True, stop=True),
                            waits=wt2)
                        ph.emit("tensor", lambda b, qs=qs, qw=qw: b.matmul(
                            csb1[0:1, 0:qw], onescol128[64:128, 0:1],
                            qsq[64:128, qs:qs + qw], start=True, stop=True))
                        ph.emit("vector", lambda b, qs=qs, qw=qw:
                                b.tensor_scalar(
                                    negcA[0:1, qs:qs + qw], csb0[0:1, 0:qw],
                                    mk2A[0:1, 0:1], -1.0 / 16.0,
                                    AluOpType.add, AluOpType.mult),
                                waits=["tensor"])
                        ph.emit("vector", lambda b, qs=qs, qw=qw:
                                b.tensor_scalar(
                                    negcB[0:1, qs:qs + qw], csb1[0:1, 0:qw],
                                    mk2B[0:1, 0:1], -1.0 / 16.0,
                                    AluOpType.add, AluOpType.mult))
                        evac_done[8 + j] = ph.snap()["vector"]
                    prev_subs = None
                    prev_exps = None
                    prev_avs = None
                    norm_mark = {}
                    prev_norm_g = {"v": ph.snap()["vector"]}
                    for qbi, (qs, qw) in enumerate(QB_LIST):
                        cb = cbcAB[:, (qbi % 2) * 1024:(qbi % 2) * 1024 + 1024]
                        ob = (obank, bcb)[qbi % 2]
                        wtc = {"vector": (prev_subs[1] if prev_subs
                                          else ph.snap()["vector"]),
                               "scalar": ph.snap()["scalar"]}
                        ph.emit("tensor", lambda b, qs=qs, qw=qw: b.matmul(
                            sAB1[:, 0:qw], onesrow[0:1, :],
                            negcA[0:1, qs:qs + qw],
                            start=True, stop=True), waits=wtc)
                        ph.emit("scalar", lambda b, qw=qw, cb=cb: b.copy(
                            out=cb[:, 0:qw], in_=sAB1[:, 0:qw]),
                            waits=["tensor"])
                        ph.emit("tensor", lambda b, qs=qs, qw=qw: b.matmul(
                            sAB1[:, 512:512 + qw], onesrow[0:1, :],
                            negcB[0:1, qs:qs + qw],
                            start=True, stop=True))
                        ph.emit("scalar", lambda b, qw=qw, cb=cb: b.copy(
                            out=cb[:, 512:512 + qw], in_=sAB1[:, 512:512 + qw]),
                            waits=["tensor"])
                        cbc_mark = ph.snap()["scalar"]
                        S_mm, S_sub, S_exp, S_sac = {}, {}, {}, {}
                        S_avA, S_avB = {}, {}
                        prev_store = ph.snap()["gpsimd"]

                        for j in range(32):
                            slot = j % 2
                            bk = (sAB0, sAB1)[slot]
                            if j >= 2:
                                bkfree = S_sub[j - 2]
                            elif prev_subs:
                                bkfree = prev_subs[j]
                            else:
                                bkfree = 0
                            wt2 = {"scalar": cbc_mark}
                            if bkfree:
                                wt2["vector"] = bkfree
                            ph.emit("tensor", lambda b, j=j, bk=bk, kof=kof,
                                    qof=qof, qs=qs, qw=qw: b.matmul(
                                        bk[:, 0:qw],
                                        ktile[0:64, kof + j * 128:kof + (j + 1) * 128],
                                        qtile[0:64, qof + qs:qof + qs + qw],
                                        start=True, stop=True,
                                        tile_position=(0, 0)), waits=wt2)
                            ph.emit("tensor", lambda b, j=j, bk=bk, kof=kof,
                                    qof=qof, qs=qs, qw=qw: b.matmul(
                                        bk[:, 512:512 + qw],
                                        ktile[64:128, kof + j * 128:kof + (j + 1) * 128],
                                        qtile[64:128, qof + qs:qof + qs + qw],
                                        start=True, stop=True,
                                        tile_position=(64, 0)))
                            S_mm[j] = ph.snap()["tensor"]
                            wt3 = {"tensor": S_mm[j]}
                            if j >= 2:
                                wt3["scalar"] = S_exp[j - 2]
                            elif prev_exps:
                                wt3["scalar"] = prev_exps[j]
                            ph.emit("vector", lambda b, j=j, bk=bk, qw=qw,
                                    cb=cb: b.scalar_tensor_tensor(
                                        abview(ssub, 2 * (j % 2) + 0, qw, 512)
                                        if False else
                                        ssub[:, (j % 2) * 1024:(j % 2 + 1) * 1024]
                                        .rearrange("p (h q) -> p h q", q=512)[:, :, 0:qw],
                                        bk[:, 0:1024]
                                        .rearrange("p (h q) -> p h q", q=512)[:, :, 0:qw],
                                        0.125,
                                        cb[:, 0:1024]
                                        .rearrange("p (h q) -> p h q", q=512)[:, :, 0:qw],
                                        AluOpType.mult, AluOpType.add),
                                    waits=wt3, dr=False)
                            S_sub[j] = ph.snap()["vector"]
                            wt4 = {"vector": S_sub[j]}
                            if j >= 2:
                                wt4["tensor"] = S_avB[j - 2]
                            elif prev_avs:
                                wt4["tensor"] = prev_avs[j]
                            ph.emit("scalar", lambda b, j=j, qw=qw: b.activation(
                                pring[:, (j % 2) * 1024:(j % 2 + 1) * 1024]
                                .rearrange("p (h q) -> p h q", q=512)[:, :, 0:qw],
                                ssub[:, (j % 2) * 1024:(j % 2 + 1) * 1024]
                                .rearrange("p (h q) -> p h q", q=512)[:, :, 0:qw],
                                AF.Exp), waits=wt4, dr=False)
                            S_exp[j] = ph.snap()["scalar"]
                            if j == 0:
                                ph.emit("vector", lambda b, qw=qw: b.tensor_copy(
                                    sacc[:, 0:1024]
                                    .rearrange("p (h q) -> p h q", q=512)[:, :, 0:qw],
                                    pring[:, 0:1024]
                                    .rearrange("p (h q) -> p h q", q=512)[:, :, 0:qw]),
                                    waits={"scalar": S_exp[0],
                                           "tensor": prev_sums_l[0]}, dr=False)
                            else:
                                ph.emit("vector", lambda b, j=j, qw=qw:
                                        b.tensor_tensor(
                                            sacc[:, 0:1024]
                                            .rearrange("p (h q) -> p h q", q=512)[:, :, 0:qw],
                                            sacc[:, 0:1024]
                                            .rearrange("p (h q) -> p h q", q=512)[:, :, 0:qw],
                                            pring[:, (j % 2) * 1024:(j % 2 + 1) * 1024]
                                            .rearrange("p (h q) -> p h q", q=512)[:, :, 0:qw],
                                            AluOpType.add),
                                        waits={"scalar": S_exp[j],
                                               "vector": S_sac[j - 1]}, dr=False)
                            S_sac[j] = ph.snap()["vector"]
                            if j >= 1:
                                jj = j - 1
                                wt5 = {"scalar": S_exp[jj]}
                                if jj == 0:
                                    wt5["vector"] = prev_norm_g["v"]
                                ph.emit("tensor", lambda b, jj=jj, vof=vof, qw=qw,
                                        ob=ob: b.matmul(
                                            ob[0:64, 0:qw],
                                            vtileA[:, vof + jj * 64:vof + (jj + 1) * 64],
                                            pring[:, (jj % 2) * 1024:(jj % 2) * 1024 + qw],
                                            start=(jj == 0), stop=False,
                                            tile_position=(0, 0),
                                            skip_group_check=True), waits=wt5)
                                S_avA[jj] = ph.snap()["tensor"]
                                ph.emit("tensor", lambda b, jj=jj, vof=vof, qw=qw,
                                        ob=ob: b.matmul(
                                            ob[64:128, 0:qw],
                                            vtileB[:, vof + jj * 64:vof + (jj + 1) * 64],
                                            pring[:, (jj % 2) * 1024 + 512:
                                                  (jj % 2) * 1024 + 512 + qw],
                                            start=(jj == 0), stop=False,
                                            tile_position=(0, 64),
                                            skip_group_check=True))
                                S_avB[jj] = ph.snap()["tensor"]
                        for jj in (31,):
                            ph.emit("tensor", lambda b, jj=jj, vof=vof, qw=qw:
                                    b.matmul(
                                        obank[0:64, 0:qw],
                                        vtileA[:, vof + jj * 64:vof + (jj + 1) * 64],
                                        pring[:, (jj % 2) * 1024:(jj % 2) * 1024 + qw],
                                        start=False, stop=True,
                                        tile_position=(0, 0),
                                        skip_group_check=True),
                                    waits={"scalar": S_exp[jj]})
                            S_avA[jj] = ph.snap()["tensor"]
                            ph.emit("tensor", lambda b, jj=jj, vof=vof, qw=qw:
                                    b.matmul(
                                        obank[64:128, 0:qw],
                                        vtileB[:, vof + jj * 64:vof + (jj + 1) * 64],
                                        pring[:, (jj % 2) * 1024 + 512:
                                              (jj % 2) * 1024 + 512 + qw],
                                        start=False, stop=True,
                                        tile_position=(0, 64),
                                        skip_group_check=True))
                            S_avB[jj] = ph.snap()["tensor"]
                        ph.emit("tensor", lambda b, qw=qw: b.matmul(
                            csb0[0:1, 0:qw], onescol128[:, 0:1], sacc[:, 0:qw],
                            start=True, stop=True),
                            waits={"vector": S_sac[31],
                                   "scalar": ph.snap()["scalar"]})
                        ph.emit("tensor", lambda b, qw=qw: b.matmul(
                            csb1[0:1, 0:qw], onescol128[:, 0:1],
                            sacc[:, 512:512 + qw],
                            start=True, stop=True))
                        prev_sums_l[0] = ph.snap()["tensor"]
                        ph.emit("vector", lambda b, qw=qw: b.reciprocal(
                            rrowA[0:1, 0:qw], csb0[0:1, 0:qw]),
                            waits={"tensor": prev_sums_l[0]})
                        ph.emit("vector", lambda b, qs=qs, qw=qw: b.tensor_tensor(
                            rrowA[0:1, 0:qw], rrowA[0:1, 0:qw],
                            qmask_sb[0:1, qs - 1:qs - 1 + qw], AluOpType.mult))
                        ph.emit("vector", lambda b, qw=qw: b.reciprocal(
                            rrowB[0:1, 0:qw], csb1[0:1, 0:qw]))
                        ph.emit("vector", lambda b, qs=qs, qw=qw: b.tensor_tensor(
                            rrowB[0:1, 0:qw], rrowB[0:1, 0:qw],
                            qmask_sb[0:1, qs - 1:qs - 1 + qw], AluOpType.mult))
                        ph.emit("tensor", lambda b, qw=qw: b.matmul(
                            csb0[0:64, 0:qw], onesrow[0:1, 0:64],
                            rrowA[0:1, 0:qw],
                            start=True, stop=True, tile_position=(0, 0),
                            skip_group_check=True), waits=["vector"])
                        ph.emit("tensor", lambda b, qw=qw: b.matmul(
                            csb1[64:128, 0:qw], onesrow[0:1, 0:64],
                            rrowB[0:1, 0:qw],
                            start=True, stop=True, tile_position=(0, 64),
                            skip_group_check=True))
                        ph.emit("scalar", lambda b, qw=qw: b.copy(
                            out=rbc_sb[0:64, 0:qw], in_=csb0[0:64, 0:qw]),
                            waits=["tensor"])
                        ph.emit("scalar", lambda b, qw=qw: b.copy(
                            out=rbc_sb[64:128, 0:qw], in_=csb1[64:128, 0:qw]))
                        ph.emit("vector", lambda b, qw=qw, ob=ob: b.tensor_tensor(
                            ostg[:, 0:qw], ob[:, 0:qw], rbc_sb[:, 0:qw],
                            AluOpType.mult),
                            waits={"scalar": ph.snap()["scalar"],
                                   "gpsimd": prev_store,
                                   "tensor": S_avB[31]})
                        norm_mark[qbi] = ph.snap()["vector"]
                        ph.emit("gpsimd", lambda b, hp=hp, qs=qs, qw=qw: b.dma_start(
                            out=oscr[hp * 128:(hp + 1) * 128, qs - 1:qs - 1 + qw],
                            in_=ostg[:, 0:qw]), waits=["vector"])
                        prev_subs = (S_sub[30], S_sub[31])
                        prev_exps = (S_exp[30], S_exp[31])
                        prev_avs = (S_avB[30], S_avB[31])
                        prev_norm_g["v"] = norm_mark.get(qbi - 1, prev_norm_g["v"])
                    pair_done[hp] = ph.snap()["tensor"]
                ph.realize(nc, blk, sems)

        with ExitStack() as st:
            otile = _sbuf(st, nc, "c_o", [128, NCH * 512])
            xtile = _sbuf(st, nc, "c_x", [128, NCH * 512])
            x2tile = _sbuf(st, nc, "c_x2", [128, NCH * 512])
            wo_sb = _sbuf(st, nc, "c_wo", [128, NCH * D])
            bP0 = _psum(st, nc, "c_bP0", [128, 512])
            bP1 = _psum(st, nc, "c_bP1", [128, 512])
            sems = _sems(st, nc, "pd")
            with nc.Block("wo") as blk:
                ph = Ph()
                for c in range(NCH):
                    ph.emit("sync", lambda b, c=c: b.dma_start(
                        out=wo_sb[:, c * D:(c + 1) * D],
                        in_=wts["woT"][c * 128:(c + 1) * 128, :]))
                for (s, w) in F3_TILES:
                    ld_prev = ph.snap()
                    for c in range(NCH):
                        ph.emit("sync", lambda b, c=c, s=s, w=w: b.dma_start(
                            out=otile[:, c * w:(c + 1) * w],
                            in_=oscr[c * 128:(c + 1) * 128, s:s + w]),
                            waits={"tensor": ld_prev["tensor"]})
                        ph.emit("sync", lambda b, c=c, s=s, w=w: b.dma_start(
                            out=xtile[:, c * w:(c + 1) * w],
                            in_=xres[c * 128:(c + 1) * 128, 1 + s:1 + s + w]),
                            waits={"vector": ld_prev["vector"]})

                    def wo_epi(o, bank, s=s, w=w):
                        ph.emit("vector", lambda b, o=o, bank=bank, w=w:
                                b.tensor_tensor(
                                    x2tile[:, o * w:(o + 1) * w], bank[:, 0:w],
                                    xtile[:, o * w:(o + 1) * w], AluOpType.add),
                                waits=["tensor", "gpsimd"])
                        ph.emit("gpsimd", lambda b, o=o, s=s, w=w: b.dma_start(
                            out=x2scr[o * 128:(o + 1) * 128, s:s + w],
                            in_=x2tile[:, o * w:(o + 1) * w]), waits=["vector"])
                    ph.emit("vector", lambda b: b.memset(x2tile[0:1, 0:1], 0.0),
                            waits=["sync"])
                    _emit_proj(ph, wo_sb, otile, w, (bP0, bP1), wo_epi, "vector")
                ph.realize(nc, blk, sems)

        with ExitStack() as st:
            xt = _sbuf(st, nc, "d_xt", [128, 2 * NCH * 410])
            ln1 = _sbuf(st, nc, "d_ln", [128, NCH * 411])
            xr = _sbuf(st, nc, "d_xr", [128, NCH * 410])
            xk = _sbuf(st, nc, "d_xk", [128, NCH * 410])
            rti = _sbuf(st, nc, "d_r", [128, NCH * 410])
            ka = _sbuf(st, nc, "d_ka", [128, NCH * 410])
            tmp = _sbuf(st, nc, "d_tmp", [128, 2 * 512])
            rows = _sbuf(st, nc, "d_rows", [1, 4 * 512])
            wr_sb = _sbuf(st, nc, "d_wr", [128, NCH * D])
            wk_sb = _sbuf(st, nc, "d_wk", [128, NCH * D])
            wv_sb = _sbuf(st, nc, "d_wv", [128, NCH * D])
            bS1 = _psum(st, nc, "d_bS1", [128, 512])
            bS2 = _psum(st, nc, "d_bS2", [128, 512])
            bA = _psum(st, nc, "d_bA", [128, 512])
            bB = _psum(st, nc, "d_bB", [128, 512])
            bP0 = _psum(st, nc, "d_bP0", [128, 512])
            bP1 = _psum(st, nc, "d_bP1", [128, 512])
            sems = _sems(st, nc, "pe")
            with nc.Block("ffn3") as blk:
                ph = Ph()
                _ffn_phase(ph, x2scr, y, F3_TILES, 1, NQ,
                           xt, ln1, xr, xk, rti, ka, tmp, rows,
                           wr_sb, wk_sb, wv_sb, wts["wrT"], wts["wkfT"],
                           wts["wvfT"], bS1, bS2, bA, bB, bP0, bP1,
                           onesrow, onescol128, murs, muks, m8, out_shift=1)
                ph.realize(nc, blk, sems)

    return nc


def _host_prepare(inputs):
    x = np.asarray(inputs["L0_attn_x"], dtype=np.float32)
    wT = {}
    for nm, key in (("wqT", "L0_attn_Wq"), ("wkT", "L0_attn_Wk"),
                    ("wvT", "L0_attn_Wv"), ("woT", "L0_attn_Wo"),
                    ("wrT", "L0_ffn_W_r"), ("wkfT", "L0_ffn_W_k"),
                    ("wvfT", "L0_ffn_W_v")):
        wT[nm] = np.ascontiguousarray(np.asarray(inputs[key], np.float32).T)
    mur = np.ascontiguousarray(
        np.asarray(inputs["L0_ffn_mu_r"], np.float32).reshape(NCH, 128).T)
    muk = np.ascontiguousarray(
        np.asarray(inputs["L0_ffn_mu_k"], np.float32).reshape(NCH, 128).T)
    in_maps = []
    for core in range(8):
        b, h = core // 2, core % 2
        xf = np.ascontiguousarray(x[b].T)
        half = [xf[:, 0:2048], xf[:, 2048:4096]]
        ctx = [np.zeros((D, 2), np.float32), np.ascontiguousarray(xf[:, 2046:2048])]
        own, oth = h, 1 - h
        segA = np.concatenate([ctx[own], half[own]], axis=1)
        segB = np.concatenate([ctx[oth], half[oth]], axis=1)
        xin = np.ascontiguousarray(np.concatenate([segA, segB], axis=1))
        qmask = np.ones((1, NQ), np.float32)
        if h == 0:
            qmask[0, 0] = 0.0
        im = {"xin": xin, "mur": mur, "muk": muk, "qmask": qmask}
        im.update(wT)
        in_maps.append(im)
    return in_maps


def kernel(**inputs):
    in_maps = _host_prepare(inputs)
    nc = build_nc()
    res = run_bass_kernel_spmd(nc, in_maps, core_ids=list(range(8)))
    out = np.empty((4, 4096, 1024), np.float32)
    for core in range(8):
        b, h = core // 2, core % 2
        out[b, 2048 * h:2048 * (h + 1), :] = res.results[core]["y"].T
    return out


if __name__ == "__main__":
    import time
    t0 = time.time()
    nc = build_nc()
    print(f"built ok in {time.time() - t0:.1f}s")
